# version 16
# baseline (speedup 1.0000x reference)
import os
import numpy as np
import ml_dtypes
import concourse.bass as bass
import concourse.mybir as mybir
from concourse.bass_utils import run_bass_kernel_spmd
from contextlib import ExitStack

F32 = mybir.dt.float32
BF16 = mybir.dt.bfloat16
I32 = mybir.dt.int32
AF = mybir.ActivationFunctionType
ALU = mybir.AluOpType

NCORES = 8
D = 1024
S = 2048
NDT = D // 128
NCH = S // 512
HID = 3072
NHT = HID // 128
NP = 5
NJ = 8

CLIP_HI = float(np.float32(1.0) - np.float32(1e-6))
LN_EPS = 1e-5
EPS_DEN = 1e-7
INV127 = float(np.float32(1.0) / np.float32(127.0))


def _beta():
    j = np.arange(6, dtype=np.float32)
    alpha = (j + 1.0) ** (-1.5)
    tail = np.flip(np.cumsum(np.flip(alpha)))
    beta = np.concatenate([np.zeros(1, np.float32), tail[1:].astype(np.float32),
                           np.zeros(5, np.float32)])
    beta = beta / beta.sum()
    return [float(beta[p]) for p in range(1, 6)]

BETA = _beta()


class Plan:
    ENGS = ("sync", "scalar", "vector", "tensor", "gpsimd")

    def __init__(self, nc):
        self.nc = nc
        self.ops = {e: [] for e in self.ENGS}
        self.sems = {}
        self.barn = 0
        self.region = None
        self.ifstack = {e: [] for e in self.ENGS}

    def sem(self, name):
        if name not in self.sems:
            h = self.nc.ctx.enter_context(self.nc.semaphore(name))
            self.sems[name] = [h, 0]
        return self.sems[name]

    def raw(self, eng, build):
        self.ops[eng].append(lambda E, b=build: b(E))

    def inc(self, eng, build, sem, k=1):
        s = self.sem(sem)
        s[1] += k
        if self.region is not None:
            r = self.region.setdefault(eng, {})
            r[sem] = r.get(sem, 0) + k
        self.ops[eng].append(lambda E, b=build, h=s[0], kk=k: b(E).then_inc(h, kk))
        return s[1]

    def guard_begin(self, engs, regs):
        assert self.region is None
        self.region = {}
        self._guard_engs = list(engs)
        for e in engs:
            st = self.ifstack[e]
            def open_(E, r=regs[e], st=st):
                g = E.If(r)
                g.__enter__()
                st.append(g)
            self.ops[e].append(open_)

    def guard_end(self):
        region = self.region
        self.region = None
        for e in self._guard_engs:
            st = self.ifstack[e]
            incs = region.get(e, {})
            def close(E, st=st, incs=dict(incs)):
                st.pop().__exit__(None, None, None)
                if incs:
                    eg = E.Else()
                    eg.__enter__()
                    for sem, k in incs.items():
                        E.sem_inc(self.sems[sem][0], k)
                    eg.__exit__(None, None, None)
            self.ops[e].append(close)
        return region

    def wait(self, eng, sem, val):
        if val <= 0:
            return
        s = self.sem(sem)
        self.ops[eng].append(lambda E, h=s[0], v=val: E.wait_ge(h, v))

    def val(self, sem):
        return self.sem(sem)[1]

    def barrier(self, bcell_v, bcell_s):
        self.barn += 1
        tgt = 5 * self.barn
        s = self.sem("bar")
        self.ops["tensor"].append(lambda E, h=s[0]: E.nop().then_inc(h, 1))
        self.ops["sync"].append(lambda E, h=s[0]: E.nop().then_inc(h, 1))
        self.ops["gpsimd"].append(lambda E, h=s[0]: E.engine_nop().then_inc(h, 1))
        self.ops["vector"].append(
            lambda E, h=s[0], c=bcell_v: E.tensor_copy(c, c).then_inc(h, 1))
        self.ops["scalar"].append(
            lambda E, h=s[0], c=bcell_s: E.copy(out=c, in_=c).then_inc(h, 1))
        s[1] = tgt
        for e in self.ENGS:
            self.wait(e, "bar", tgt)

    def realize(self, block):
        for eng in self.ENGS:
            def run(E, ops=self.ops[eng]):
                for f in ops:
                    f(E)
            getattr(block, eng)(run)


def build_program(mode, stop_after=None):
    fused = mode == "fused"
    main = mode in ("main", "fused")
    nc = bass.Bass("TRN2", target_bir_lowering=False, detect_race_conditions=False, dynamic_dma_scratch_size=16384)

    def din(name, shape, dt=F32):
        return nc.dram_tensor(name, shape, dt, kind="ExternalInput")

    xT = din("xT", [D, S])
    consts = din("consts", [128, 8])
    cbf = din("cbf", [128, 4], BF16)
    onesrow = din("onesrow", [1, 128])
    if main:
        w_qkv = din("w_qkv", [24, D, 128])
        w_out = din("w_out", [NDT, D, 128])
        w_g = din("w_g", [NHT, D, 128], I32)
        w_u = din("w_u", [NHT, D, 128], I32)
        w_d = din("w_d", [NDT, HID, 128], I32)
        sel = din("sel", [16, NJ * 128], BF16)
        if not fused:
            all_tot = din("all_tot", [NCORES * 128, 40])
            all_ks = din("all_ks", [NCORES * 2, 40])
            mask = din("mask", [128, NCORES])
            ksmask = din("ksmask", [2, NCORES])
        else:
            xRT = din("xRT", [D, 12 * 512])
            rparam = din("rparam", [1, 8], I32)
            shR_kv = [nc.dram_tensor(f"shR_kv{i}", [128, 40], F32,
                                     kind="Internal", addr_space="Shared")
                      for i in range(2)]
            shR_ks = [nc.dram_tensor(f"shR_ks{i}", [2, 40], F32,
                                     kind="Internal", addr_space="Shared")
                      for i in range(2)]
            shRf = nc.dram_tensor("shRf", [1, 2], I32, kind="Internal",
                                  addr_space="Shared")
            shSS = nc.dram_tensor("shSS", [128, 40], F32, kind="Internal",
                                  addr_space="Shared")
            shSSf = nc.dram_tensor("shSSf", [1, NJ], I32, kind="Internal",
                                   addr_space="Shared")
            shKS = nc.dram_tensor("shKS", [16, NP], F32, kind="Internal",
                                  addr_space="Shared")
            shKSf = nc.dram_tensor("shKSf", [1, 1], I32, kind="Internal",
                                   addr_space="Shared")
        o_y = nc.dram_tensor("yT", [D, S], F32, kind="ExternalOutput")
        kscr = nc.dram_tensor("kscr", [16, NP, S], BF16, kind="Internal")
        qscr = nc.dram_tensor("qscr", [16, NP, S], BF16, kind="Internal")
    else:
        w_kv = din("w_kv", [16, D, 128])
        o_tot = nc.dram_tensor("tot", [128, 40], F32, kind="ExternalOutput")
        o_kst = nc.dram_tensor("kst", [2, 40], F32, kind="ExternalOutput")

    sb = lambda cx, name, shape, dt=F32: cx.enter_context(
        nc.sbuf_tensor(name, shape, dt))
    P = Plan(nc)

    cx0 = nc.ctx
    sc = sb(cx0, "sc", [128, 8])
    scb = sb(cx0, "scb", [128, 4], BF16)
    sonesrow = sb(cx0, "sonesrow", [1, 128])
    xn = [sb(cx0, f"xn{i}", [128, S], BF16) for i in range(NDT)]
    wq = [sb(cx0, f"wq{i}", [128, NDT, 128], BF16) for i in range(2)]
    tot_kv = sb(cx0, "tot_kv", [128, 40])
    bcv = sb(cx0, "bcv", [1, 1])
    bcs = sb(cx0, "bcs", [1, 1])
    if main:
        num = [sb(cx0, f"num{j}", [128, S], BF16) for j in range(NJ)]
        ksc16 = sb(cx0, "ksc16", [16, NP])
        sel_sb = sb(cx0, "sel_sb", [16, NJ * 128], BF16)
        if not fused:
            stot = sb(cx0, "stot", [128, NCORES, 40])
            sks = sb(cx0, "sks", [2, NCORES, 40])
            smask = sb(cx0, "smask", [128, NCORES])
            sksmask = sb(cx0, "sksmask", [2, NCORES])
            cacc = [sb(cx0, f"cacc{i}", [128, 40]) for i in range(2)]
            kacc = [sb(cx0, f"kacc{i}", [2, 40]) for i in range(2)]
        else:
            carry_src = sb(cx0, "carry_src", [128, 40])
            totR = sb(cx0, "totR", [128, 40])
            colR = sb(cx0, "colR", [128, 40])
            colRb = sb(cx0, "colRb", [128, 40], BF16)
            totc = sb(cx0, "totc", [128, 40])
            colc = sb(cx0, "colc", [128, 40])
            kstR = sb(cx0, "kstR", [2, 40])
            kstR2 = sb(cx0, "kstR2", [2, 40])
            tmpkv = sb(cx0, "tmpkv", [128, 40])
            tmpks = sb(cx0, "tmpks", [2, 40])
            ksfin = sb(cx0, "ksfin", [16, NP])
            sparam = sb(cx0, "sparam", [1, 8], I32)
            spoll = sb(cx0, "spoll", [1, 1], I32)
            stok32 = sb(cx0, "stok32", [1, 1], I32)
    else:
        colT = sb(cx0, "colT", [128, 8])
        colTb = sb(cx0, "colTb", [128, 8], BF16)
        kst2 = sb(cx0, "kst2", [2, 40])
        totx = sb(cx0, "totx", [128, 4])

    pb = [nc.ctx.enter_context(nc.psum_tensor(f"pb{i}", [128, 512], F32))
          for i in range(8)]

    bcell_v = bcv[0:1, 0:1]
    bcell_s = bcs[0:1, 0:1]
    c0 = lambda i, p=128: sc[0:p, i:i + 1]

    def ld(dst, src, sem="d_misc"):
        return P.inc("sync", lambda E, d=dst, s=src: E.dma_start(out=d, in_=s),
                     sem, 16)

    ld(sc[:, :], consts[:, :])
    ld(scb[:, :], cbf[:, :])
    ld(sonesrow[:, :], onesrow[:, :])
    misc_n = 3
    if main:
        ld(sel_sb[:, :], sel[:, :])
        misc_n += 1
        if not fused:
            ld(stot[:, :, :], all_tot.ap().rearrange("(r p) c -> p r c", p=128))
            ld(sks[:, :, :], all_ks.ap().rearrange("(r p) c -> p r c", p=2))
            ld(smask[:, :], mask[:, :])
            ld(sksmask[:, :], ksmask[:, :])
            misc_n += 4
        else:
            ld(sparam[:, :], rparam[:, :])
            misc_n += 1
    P.raw("vector", lambda E: E.memset(bcv[:, :], 0.0))
    P.raw("scalar", lambda E: E.memzero(bcs[:, :]))

    wlast = {}

    def wload(w, t, buf, nt=NDT):
        key = id(buf)
        if key in wlast:
            P.wait("gpsimd", wlast[key][0], wlast[key][1])
        return P.inc("gpsimd", lambda E, w=w, t=t, b=buf, nt=nt: E.dma_start(
            out=b[:, :, :],
            in_=w.ap()[t].rearrange("(dt p) e -> p dt e", p=128)), "d_w", 16)

    if fused:
        GENGS = ("sync", "scalar", "vector", "tensor")
        regs_cnt, regs_g = {}, {}
        regs_par, regs_npar = {}, {}
        eng_of = {"sync": nc.sync, "scalar": nc.scalar, "vector": nc.vector,
                  "tensor": nc.tensor, "gpsimd": nc.gpsimd}
        for e in GENGS:
            regs_cnt[e] = eng_of[e].alloc_register(f"rcnt_{e}")
            regs_g[e] = eng_of[e].alloc_register(f"rg_{e}")
        for e in ("sync", "vector"):
            regs_par[e] = eng_of[e].alloc_register(f"rpar_{e}")
            regs_npar[e] = eng_of[e].alloc_register(f"rnp_{e}")
        r_f = nc.sync.alloc_register("r_f")
        r_go = nc.sync.alloc_register("r_go")
        r_pw = nc.sync.alloc_register("r_pw")
        r_tok = nc.sync.alloc_register("r_tok")

        def poll_fn(flag_ap):
            P.raw("sync", lambda E: E.reg_mov(r_go, 1))
            def loop(E, flag_ap=flag_ap):
                with E.While(r_go):
                    E.dma_start(out=spoll[:, :], in_=flag_ap).then_inc(
                        P.sem("r_pd")[0], 16)
                    E.reg_add(r_pw, r_pw, 16)
                    E.wait_ge(P.sem("r_pd")[0], r_pw)
                    E.reg_load(r_f, spoll[0:1, 0:1])
                    E.reg_alu(r_go, r_f, r_tok, ALU.not_equal)
            P.raw("sync", loop)

    def emit_r_phase_and_exchange():
        NRT = 12
        for e in GENGS:
            P.wait(e, "d_misc", misc_n * 16)
            P.raw(e, lambda E, r=regs_cnt[e]: E.reg_load(r, sparam[0:1, 0:1]))
        for e in ("sync", "vector"):
            P.raw(e, lambda E, r=regs_par[e]: E.reg_load(r, sparam[0:1, 1:2]))
            P.raw(e, lambda E, ri=regs_par[e], ro=regs_npar[e]: E.reg_alu(
                ro, ri, 0, ALU.is_equal))
        P.raw("sync", lambda E: E.reg_load(r_tok, sparam[0:1, 2:3]))
        P.raw("sync", lambda E: E.reg_mov(r_pw, 0))
        poll = poll_fn

        cxR = ExitStack()
        wkv = sb(cxR, "wkv", [128, 16, NDT, 128], BF16)
        sxR = sb(cxR, "sxR", [128, NDT, 512])
        xnR = sb(cxR, "xnR", [128, NDT, 512], BF16)
        rxb = [sb(cxR, f"rxb{i}", [128, 512], BF16) for i in range(2)]
        rx2 = [sb(cxR, f"rx2{i}", [128, 512], BF16) for i in range(2)]
        mR = sb(cxR, "mR", [1, 512]); e2R = sb(cxR, "e2R", [1, 512])
        vR = sb(cxR, "vR", [1, 512]); sR = sb(cxR, "sR", [1, 512])
        rsR = sb(cxR, "rsR", [1, 512])
        ekR = sb(cxR, "ekR", [128, 512], BF16)
        evR = sb(cxR, "evR", [128, 512], BF16)
        bxkR = sb(cxR, "bxkR", [128, 512], BF16)
        ykR = sb(cxR, "ykR", [128, 512], BF16)
        tkR = sb(cxR, "tkR", [128, 512], BF16)
        tmR = sb(cxR, "tmR", [128, 512], BF16)
        tvS = sb(cxR, "tvS", [128, 512], BF16)
        onepl = sb(cxR, "onepl", [128, 512], BF16)
        negpl = sb(cxR, "negpl", [128, 512], BF16)
        tbufR = sb(cxR, "tbufR", [128, 512])

        P.raw("vector", lambda E: E.memset(totR[:, :], 0.0))
        P.raw("vector", lambda E: E.memset(colR[:, :], 0.0))
        P.raw("vector", lambda E: E.memset(kstR[:, :], 0.0))
        P.raw("vector", lambda E: E.memset(totc[:, :], 0.0))
        P.raw("vector", lambda E: E.memset(colc[:, :], 0.0))
        P.raw("vector", lambda E: E.memset(onepl[:, :], 1.0))
        P.raw("vector", lambda E: E.memset(negpl[:, :], -1.0))
        P.raw("vector", lambda E: E.memset(carry_src[:, :], 0.0))
        P.raw("scalar", lambda E: E.copy(out=stok32[:, :], in_=sparam[0:1, 2:3]))
        for t in range(16):
            wkl = P.inc("gpsimd", lambda E, t=t: E.dma_start(
                out=wkv[:, t, :, :],
                in_=w_qkv.ap()[8 + t].rearrange("(dt p) e -> p dt e", p=128)),
                "r_wk", 16)
        P.wait("tensor", "r_wk", wkl)

        for i in range(NRT):
            for e in GENGS:
                P.raw(e, lambda E, r=regs_cnt[e], g=regs_g[e], i=i: E.reg_alu(
                    g, r, i, ALU.is_gt))
            P.guard_begin(GENGS, regs_g)
            xv = 0
            for dt in range(NDT):
                xv = P.inc("sync", lambda E, dt=dt, i=i: E.dma_start(
                    out=sxR[:, dt, :],
                    in_=xRT.ap()[dt * 128:(dt + 1) * 128, i * 512:(i + 1) * 512]),
                    "r_x", 16)
            for dt in range(NDT):
                b = dt % 2
                P.wait("scalar", "r_x", xv)
                if True:
                    P.wait("scalar", "r_smm", P.val("r_smm") - 1)
                P.raw("scalar", lambda E, dt=dt, b=b: E.copy(
                    out=rxb[b][:, :], in_=sxR[:, dt, :]))
                cv = P.inc("scalar", lambda E, dt=dt, b=b: E.activation(
                    out=rx2[b][:, :], in_=sxR[:, dt, :], func=AF.Square,
                    scale=c0(1), bias=c0(0)), "r_cv", 1)
                P.wait("tensor", "r_cv", cv)
                if dt == 0:
                    P.wait("tensor", "r_var", P.val("r_var"))
                P.raw("tensor", lambda E, b=b, dt=dt: E.matmul(
                    pb[0].ap()[0:1, :], scb[:, 0:1], rxb[b][:, :],
                    start=(dt == 0), stop=(dt == NDT - 1)))
                P.inc("tensor", lambda E, b=b, dt=dt: E.matmul(
                    pb[1].ap()[0:1, :], scb[:, 0:1], rx2[b][:, :],
                    start=(dt == 0), stop=(dt == NDT - 1)), "r_smm", 1)
            P.wait("vector", "r_smm", P.val("r_smm"))
            if i > 0:
                P.raw("vector", lambda E: E.tensor_add(totR[:, :], totR[:, :],
                                                       totc[:, :]))
                P.raw("vector", lambda E: E.tensor_add(colR[:, :], colR[:, :],
                                                       colc[:, :]))
            P.raw("vector", lambda E: E.tensor_scalar(mR[:, :], pb[0].ap()[0:1, :],
                                                      1.0 / D, None, ALU.mult))
            P.raw("vector", lambda E: E.tensor_scalar(e2R[:, :], pb[1].ap()[0:1, :],
                                                      1.0 / D, None, ALU.mult))
            P.raw("vector", lambda E: E.tensor_mul(vR[:, :], mR[:, :], mR[:, :]))
            P.raw("vector", lambda E: E.tensor_sub(vR[:, :], e2R[:, :], vR[:, :]))
            vv = P.inc("vector", lambda E: E.tensor_scalar(
                vR[:, :], vR[:, :], LN_EPS, None, ALU.add), "r_var", 1)
            P.wait("scalar", "r_var", vv)
            sq = P.inc("scalar", lambda E: E.activation(
                out=sR[:, :], in_=vR[:, :], func=AF.Sqrt, scale=c0(1, 1),
                bias=c0(0, 1)), "r_sqt", 1)
            P.wait("vector", "r_sqt", sq)
            rv = P.inc("vector", lambda E: E.reciprocal(rsR[:, :], sR[:, :]),
                       "r_rs", 1)
            P.wait("tensor", "r_rs", rv)
            P.wait("tensor", "r_xn", P.val("r_xn"))
            P.raw("tensor", lambda E: E.matmul(pb[2].ap()[:, :], sonesrow[:, :],
                                               mR[:, :], start=True, stop=True))
            bc = P.inc("tensor", lambda E: E.matmul(
                pb[3].ap()[:, :], sonesrow[:, :], rsR[:, :], start=True,
                stop=True), "r_bc", 1)
            P.wait("vector", "r_bc", bc)
            for dt in range(NDT):
                P.raw("vector", lambda E, dt=dt: E.tensor_sub(
                    tbufR[:, :], sxR[:, dt, :], pb[2].ap()[:, :]))
                P.raw("vector", lambda E, dt=dt: E.tensor_mul(
                    xnR[:, dt, :], tbufR[:, :], pb[3].ap()[:, :]))
            xnv = P.inc("vector", lambda E: E.tensor_copy(bcell_v, bcell_v),
                        "r_xn", 1)
            for j in range(NJ):
                bank_k = 4 + 2 * (j % 2)
                bank_v = 5 + 2 * (j % 2)
                P.wait("tensor", "r_xn", xnv)
                P.wait("tensor", "r_ev", P.val("r_ev") - 2)
                for dt in range(NDT):
                    bld = lambda E, j=j, dt=dt, bank=bank_k: E.matmul(
                        pb[bank].ap()[:, :], wkv[:, j, dt, :], xnR[:, dt, :],
                        start=(dt == 0), stop=(dt == NDT - 1))
                    P.raw("tensor", bld)
                for dt in range(NDT):
                    bld = lambda E, j=j, dt=dt, bank=bank_v: E.matmul(
                        pb[bank].ap()[:, :], wkv[:, 8 + j, dt, :], xnR[:, dt, :],
                        start=(dt == 0), stop=(dt == NDT - 1))
                    if dt == NDT - 1:
                        kvv = P.inc("tensor", bld, "r_kv", 1)
                    else:
                        P.raw("tensor", bld)
                P.wait("scalar", "r_kv", kvv)
                P.wait("scalar", "r_sq", P.val("r_sq"))
                P.raw("scalar", lambda E, bank=bank_k: E.activation(
                    out=ekR[:, :], in_=pb[bank].ap()[:, :], func=AF.Copy,
                    scale=0.125, bias=0.0))
                ev = P.inc("scalar", lambda E, bank=bank_v: E.copy(
                    out=evR[:, :], in_=pb[bank].ap()[:, :]), "r_ev", 2)
                P.wait("vector", "r_ev", ev)
                ck = P.inc("vector", lambda E: E.tensor_scalar(
                    bxkR[:, :], ekR[:, :], CLIP_HI, -CLIP_HI, ALU.min, ALU.max),
                    "r_ck", 1)
                P.wait("scalar", "r_ck", ck)
                sqv = P.inc("scalar", lambda E: E.activation(
                    out=ykR[:, :], in_=bxkR[:, :], func=AF.Square, scale=c0(1),
                    bias=c0(0)), "r_sq", 1)
                P.wait("vector", "r_sq", sqv)
                for p in range(1, 6):
                    cc = colc[:, j * NP + p - 1:j * NP + p]
                    tc = totc[:, j * NP + p - 1:j * NP + p]
                    if p == 1:
                        P.raw("vector", lambda E, A=cc: E.scalar_tensor_tensor(
                            tkR[:, :], bxkR[:, :], 1.0, onepl[:, :], ALU.mult,
                            ALU.mult, accum_out=A))
                    elif p == 2:
                        P.raw("vector", lambda E, A=cc: E.scalar_tensor_tensor(
                            tkR[:, :], ykR[:, :], 2.0, negpl[:, :], ALU.mult,
                            ALU.add, accum_out=A))
                    elif p == 3:
                        P.raw("vector", lambda E: E.tensor_scalar(
                            tmR[:, :], ykR[:, :], 4.0, -3.0, ALU.mult, ALU.add))
                        P.raw("vector", lambda E, A=cc: E.scalar_tensor_tensor(
                            tkR[:, :], tmR[:, :], 1.0, bxkR[:, :], ALU.mult,
                            ALU.mult, accum_out=A))
                    elif p == 4:
                        P.raw("vector", lambda E: E.tensor_scalar(
                            tmR[:, :], ykR[:, :], -1.0, 8.0, ALU.add, ALU.mult))
                        P.raw("vector", lambda E: E.tensor_mul(
                            tmR[:, :], tmR[:, :], ykR[:, :]))
                        P.raw("vector", lambda E, A=cc: E.scalar_tensor_tensor(
                            tkR[:, :], tmR[:, :], 1.0, onepl[:, :], ALU.mult,
                            ALU.add, accum_out=A))
                    elif p == 5:
                        P.raw("vector", lambda E: E.tensor_scalar(
                            tmR[:, :], ykR[:, :], 16.0, -20.0, ALU.mult, ALU.add))
                        P.raw("vector", lambda E: E.tensor_mul(
                            tmR[:, :], tmR[:, :], ykR[:, :]))
                        P.raw("vector", lambda E: E.tensor_scalar(
                            tmR[:, :], tmR[:, :], 5.0, None, ALU.add))
                        P.raw("vector", lambda E, A=cc: E.scalar_tensor_tensor(
                            tkR[:, :], tmR[:, :], 1.0, bxkR[:, :], ALU.mult,
                            ALU.mult, accum_out=A))
                    P.raw("vector", lambda E, A=tc: E.scalar_tensor_tensor(
                        tvS[:, :], tkR[:, :], 1.0, evR[:, :], ALU.mult, ALU.mult,
                        accum_out=A))
            tva = P.inc("vector", lambda E: E.tensor_copy(bcell_v, bcell_v),
                        "r_tva", 1)
            P.wait("scalar", "r_tva", tva)
            tvb = P.inc("scalar", lambda E: E.copy(out=bcell_s, in_=bcell_s),
                        "r_tvb", 1)
            P.wait("vector", "r_tvb", tvb)
            P.guard_end()
        P.raw("vector", lambda E: E.tensor_add(totR[:, :], totR[:, :],
                                               totc[:, :]))
        P.raw("vector", lambda E: E.tensor_add(colR[:, :], colR[:, :],
                                               colc[:, :]))
        cb = P.inc("vector", lambda E: E.tensor_copy(colRb[:, :], colR[:, :]),
                   "r_cb", 1)
        P.wait("tensor", "r_cb", cb)
        km = P.inc("tensor", lambda E: E.matmul(
            pb[0].ap()[0:2, 0:40], scb[:, 1:3], colRb[:, :], start=True,
            stop=True), "r_km", 1)
        P.wait("scalar", "r_km", km)
        ke = P.inc("scalar", lambda E: E.copy(out=kstR[:, :],
                                              in_=pb[0].ap()[0:2, 0:40]),
                   "r_ke", 1)
        cxR.close()

        rr = P.inc("vector", lambda E: E.tensor_copy(bcell_v, bcell_v), "r_rdy", 1)
        P.wait("sync", "r_rdy", rr)
        P.wait("sync", "r_ke", ke)
        P.guard_begin(("sync",), {"sync": regs_npar["sync"]})
        P.inc("sync", lambda E: E.dma_start(out=shR_kv[1].ap()[:, :],
                                            in_=totR[:, :]), "d_post", 16)
        pv = P.inc("sync", lambda E: E.dma_start(out=shR_ks[1].ap()[:, :],
                                                 in_=kstR[:, :]), "d_post", 16)
        P.wait("sync", "d_post", pv)
        P.inc("sync", lambda E: E.dma_start(out=shRf.ap()[0:1, 1:2],
                                            in_=stok32[:, :]), "d_post", 16)
        P.guard_end()
        P.guard_begin(("sync",), {"sync": regs_par["sync"]})
        poll(shRf.ap()[0:1, 1:2])
        P.inc("sync", lambda E: E.dma_start(out=tmpkv[:, :],
                                            in_=shR_kv[1].ap()[:, :]), "d_geto", 16)
        P.inc("sync", lambda E: E.dma_start(out=tmpks[:, :],
                                            in_=shR_ks[1].ap()[:, :]), "d_geto", 16)
        gv = P.val("d_geto")
        P.wait("sync", "d_geto", gv)
        P.inc("sync", lambda E: E.nop(), "c_part", 1)
        P.guard_end()
        cpv = P.val("c_part")
        P.guard_begin(("vector",), {"vector": regs_par["vector"]})
        P.wait("vector", "c_part", cpv)
        P.raw("vector", lambda E: E.tensor_add(carry_src[:, :],
                                               totR[:, :], tmpkv[:, :]))
        P.inc("vector", lambda E: E.tensor_add(kstR2[:, :], kstR[:, :],
                                               tmpks[:, :]), "c_carry", 1)
        P.guard_end()
        ccv = P.val("c_carry")
        P.guard_begin(("sync",), {"sync": regs_par["sync"]})
        P.wait("sync", "c_carry", ccv)
        for j in range(NJ):
            for r in range(2):
                P.inc("sync", lambda E, j=j, r=r: E.dma_start(
                    out=ksc16[2 * j + r:2 * j + r + 1, :],
                    in_=kstR2[r:r + 1, j * NP:(j + 1) * NP]), "d_ksc", 16)
        P.wait("sync", "d_ksc", P.val("d_ksc"))
        P.inc("sync", lambda E: E.nop(), "c_evcar", 1)
        P.guard_end()
        return P.val("c_evcar")

    if fused:
        EV_CARRY = emit_r_phase_and_exchange()
        carry_kv = carry_src
        P.barrier(bcell_v, bcell_s)
    if main and not fused:
        P.wait("vector", "d_misc", misc_n * 16)
        P.raw("vector", lambda E: E.memset(cacc[0][:, :], 0.0))
        P.raw("vector", lambda E: E.memset(kacc[0][:, :], 0.0))
        a = 0
        for r in range(NCORES):
            P.raw("vector", lambda E, r=r, a=a: E.scalar_tensor_tensor(
                cacc[1 - a][:, :], stot[:, r, :], smask[:, r:r + 1], cacc[a][:, :],
                ALU.mult, ALU.add))
            P.inc("vector", lambda E, r=r, a=a: E.scalar_tensor_tensor(
                kacc[1 - a][:, :], sks[:, r, :], sksmask[:, r:r + 1], kacc[a][:, :],
                ALU.mult, ALU.add), "s_carry", 1)
            a = 1 - a
        carry_kv = cacc[a]
        carry_ks2 = kacc[a]
        CARRY_V = P.val("s_carry")
        P.wait("gpsimd", "s_carry", CARRY_V)
        for j in range(NJ):
            for r in range(2):
                P.inc("gpsimd", lambda E, j=j, r=r: E.dma_start(
                    out=ksc16[2 * j + r:2 * j + r + 1, :],
                    in_=carry_ks2[r:r + 1, j * NP:(j + 1) * NP]), "d_rel", 16)
        KSC_DONE = P.val("d_rel")

    cxA = ExitStack()
    sx = [sb(cxA, f"sx{i}", [128, S]) for i in range(NDT)]
    xbf = [sb(cxA, f"xbf{i}", [128, 512], BF16) for i in range(2)]
    x2bf = [sb(cxA, f"x2bf{i}", [128, 512], BF16) for i in range(2)]
    tbuf = sb(cxA, "tbuf", [128, 512])
    mrow = sb(cxA, "mrow", [1, 512])
    e2row = sb(cxA, "e2row", [1, 512])
    vrow = sb(cxA, "vrow", [1, 512])
    srow = sb(cxA, "srow", [1, 512])
    rsrow = sb(cxA, "rsrow", [1, 512])

    xld = [ld(sx[dt][:, :], xT.ap()[dt * 128:(dt + 1) * 128, :], "d_x")
           for dt in range(NDT)]

    ln_mm = {}
    gi = 0
    for ch in range(NCH):
        for dt in range(NDT):
            b = dt % 2
            P.wait("scalar", "d_x", xld[dt])
            if gi == 0:
                P.wait("scalar", "d_misc", 3 * 16)
            if gi >= 2:
                P.wait("scalar", "t_ln", ln_mm[gi - 2])
            P.raw("scalar", lambda E, dt=dt, ch=ch, b=b: E.copy(
                out=xbf[b][:, :], in_=sx[dt][:, ch * 512:(ch + 1) * 512]))
            cv = P.inc("scalar", lambda E, dt=dt, ch=ch, b=b: E.activation(
                out=x2bf[b][:, :], in_=sx[dt][:, ch * 512:(ch + 1) * 512],
                func=AF.Square, scale=c0(1), bias=c0(0)), "s_cv", 1)
            P.wait("tensor", "s_cv", cv)
            P.raw("tensor", lambda E, dt=dt, b=b: E.matmul(
                pb[0].ap()[0:1, :], scb[:, 0:1], xbf[b][:, :],
                start=(dt == 0), stop=(dt == NDT - 1)))
            ln_mm[gi] = P.inc("tensor", lambda E, dt=dt, b=b: E.matmul(
                pb[1].ap()[0:1, :], scb[:, 0:1], x2bf[b][:, :],
                start=(dt == 0), stop=(dt == NDT - 1)), "t_ln", 1)
            gi += 1
        P.wait("vector", "t_ln", P.val("t_ln"))
        P.raw("vector", lambda E: E.tensor_scalar(mrow[:, :], pb[0].ap()[0:1, :],
                                                  1.0 / D, None, ALU.mult))
        P.raw("vector", lambda E: E.tensor_scalar(e2row[:, :], pb[1].ap()[0:1, :],
                                                  1.0 / D, None, ALU.mult))
        P.raw("vector", lambda E: E.tensor_mul(vrow[:, :], mrow[:, :], mrow[:, :]))
        P.raw("vector", lambda E: E.tensor_sub(vrow[:, :], e2row[:, :], vrow[:, :]))
        vv = P.inc("vector", lambda E: E.tensor_scalar(
            vrow[:, :], vrow[:, :], LN_EPS, None, ALU.add), "s_var", 1)
        P.wait("scalar", "s_var", vv)
        sq = P.inc("scalar", lambda E: E.activation(
            out=srow[:, :], in_=vrow[:, :], func=AF.Sqrt, scale=c0(1, 1),
            bias=c0(0, 1)), "s_sqrt", 1)
        P.wait("vector", "s_sqrt", sq)
        rv = P.inc("vector", lambda E: E.reciprocal(rsrow[:, :], srow[:, :]),
                   "s_rs", 1)
        P.wait("tensor", "s_rs", rv)
        P.raw("tensor", lambda E: E.matmul(pb[2].ap()[:, :], sonesrow[:, :],
                                           mrow[:, :], start=True, stop=True))
        bc = P.inc("tensor", lambda E: E.matmul(pb[3].ap()[:, :], sonesrow[:, :],
                                                rsrow[:, :], start=True, stop=True),
                   "t_bc", 1)
        P.wait("vector", "t_bc", bc)
        for dt in range(NDT):
            P.raw("vector", lambda E, dt=dt, ch=ch: E.tensor_sub(
                tbuf[:, :], sx[dt][:, ch * 512:(ch + 1) * 512], pb[2].ap()[:, :]))
            P.raw("vector", lambda E, dt=dt, ch=ch: E.tensor_mul(
                xn[dt][:, ch * 512:(ch + 1) * 512], tbuf[:, :], pb[3].ap()[:, :]))
        xch = P.inc("vector", lambda E: E.tensor_copy(bcell_v, bcell_v), "s_xnch", 1)
        P.wait("tensor", "s_xnch", xch)
    P.barrier(bcell_v, bcell_s)
    if main and os.environ.get("KDBG"):
        o_dbg2 = nc.dram_tensor("dbg2", [3, 128, S], BF16, kind="ExternalOutput")
        for di, buf in enumerate([xn[0], xn[7]]):
            P.inc("gpsimd", lambda E, di=di, buf=buf: E.dma_start(
                out=o_dbg2.ap()[di], in_=buf[:, :]), "d_dbg2", 16)
        P.wait("gpsimd", "d_dbg2", P.val("d_dbg2"))
    cxA.close()

    def _finish():
        with nc.Block() as block:
            P.realize(block)
        return nc

    if stop_after == "ln":
        return _finish()

    cxH = ExitStack()
    NH_IN = 15
    if main:
        hdn = [sb(cxH, f"hdn{t}", [128, S], BF16) for t in range(NH_IN)]
        wg_b = [sb(cxH, f"wg_b{i}", [128, NDT, 128], BF16) for i in range(2)]
        wu_b = [sb(cxH, f"wu_b{i}", [128, NDT, 128], BF16) for i in range(2)]
        sgb = [sb(cxH, f"sgb{i}", [128, 512]) for i in range(2)]
        ubf = [sb(cxH, f"ubf{i}", [128, 512]) for i in range(2)]
    cxB = ExitStack()
    xkf = sb(cxB, "xkf", [128, S], BF16)
    xqf = sb(cxB, "xqf", [128, S], BF16)
    ykf = sb(cxB, "ykf", [128, S], BF16)
    yqf = sb(cxB, "yqf", [128, S], BF16)
    bv = sb(cxB, "bv", [128, S], BF16)
    tkr = sb(cxB, "tkr", [128, 2, S], BF16)
    tqr = sb(cxB, "tqr", [128, 2, S], BF16)
    ctmp = sb(cxB, "ctmp", [128, 512], BF16) if not main else None
    ctv = sb(cxB, "ctv", [128, S], BF16)
    ckvp = sb(cxB, "ckvp", [128, S])
    sstate = sb(cxB, "sstate", [128, NP])
    if main:
        kqst = [sb(cxB, f"kqst{i}", [2, NCH * 512], BF16) for i in range(2)]

    def proj(buf_w, wv, dst, scale, banks):
        P.wait("tensor", "d_w", wv)
        P.wait("tensor", "s_ev", P.val("s_ev"))
        P.wait("tensor", "s_kse", P.val("s_kse"))
        P.wait("scalar", "s_tkp", P.val("s_tkp"))
        for ch in range(NCH):
            for dt in range(NDT):
                bank = banks[ch]
                bld = lambda E, b=buf_w, dt=dt, ch=ch, bank=bank: E.matmul(
                    pb[bank].ap()[:, :], b[:, dt, :],
                    xn[dt][:, ch * 512:(ch + 1) * 512],
                    start=(dt == 0), stop=(dt == NDT - 1))
                if dt == NDT - 1 and ch == NCH - 1:
                    mv = P.inc("tensor", bld, "t_mm", 1)
                else:
                    P.raw("tensor", bld)
        wlast[id(buf_w)] = ("t_mm", mv)
        P.wait("scalar", "t_mm", mv)
        ev = 0
        for ch in range(NCH):
            if scale is None:
                ev = P.inc("scalar", lambda E, ch=ch, bank=banks[ch], d=dst: E.copy(
                    out=d[:, ch * 512:(ch + 1) * 512], in_=pb[bank].ap()[:, :]),
                    "s_ev", 1)
            else:
                ev = P.inc("scalar", lambda E, ch=ch, bank=banks[ch], d=dst,
                           s=scale: E.activation(
                    out=d[:, ch * 512:(ch + 1) * 512], in_=pb[bank].ap()[:, :],
                    func=AF.Copy, scale=s, bias=0.0), "s_ev", 1)
        return ev

    def cheb(y, x, p, dst, tmp, ch, fin=None):
        sl = ((lambda t: t[:, :]) if ch is None
              else (lambda t: t[:, ch * 512:(ch + 1) * 512]))
        em = ((lambda b: P.inc("vector", b, fin[0], fin[1])) if fin
              else (lambda b: P.raw("vector", b)))
        if p == 2:
            em(lambda E: E.tensor_scalar(
                dst, sl(y), 2.0, -1.0, ALU.mult, ALU.add))
        elif p == 3:
            P.raw("vector", lambda E: E.tensor_scalar(
                tmp[:, :], sl(y), 4.0, -3.0, ALU.mult, ALU.add))
            em(lambda E: E.tensor_mul(dst, tmp[:, :], sl(x)))
        elif p == 4:
            P.raw("vector", lambda E: E.tensor_scalar(
                tmp[:, :], sl(y), -1.0, 8.0, ALU.add, ALU.mult))
            P.raw("vector", lambda E: E.tensor_mul(tmp[:, :], tmp[:, :], sl(y)))
            em(lambda E: E.tensor_scalar(
                dst, tmp[:, :], 1.0, None, ALU.add))
        elif p == 5:
            P.raw("vector", lambda E: E.tensor_scalar(
                tmp[:, :], sl(y), 16.0, -20.0, ALU.mult, ALU.add))
            P.raw("vector", lambda E: E.tensor_mul(tmp[:, :], tmp[:, :], sl(y)))
            P.raw("vector", lambda E: E.tensor_scalar(
                tmp[:, :], tmp[:, :], 5.0, None, ALU.add))
            em(lambda E: E.tensor_mul(dst, tmp[:, :], sl(x)))

    INV2 = INV127 * INV127

    def ffn_ht_post(ht):
        bi = ht % 2
        wgv = wload(w_g, ht, wg_b[bi])
        wuv = wload(w_u, ht, wu_b[bi])
        P.wait("tensor", "d_w", wgv)
        P.wait("tensor", "s_sg", P.val("s_sg"))
        P.wait("tensor", "s_rde", P.val("s_rde"))
        for ch in range(NCH):
            for dt in range(NDT):
                bld = lambda E, dt=dt, ch=ch, bi=bi: E.matmul(
                    pb[4 + ch].ap()[:, :], wg_b[bi][:, dt, :],
                    xn[dt][:, ch * 512:(ch + 1) * 512],
                    start=(dt == 0), stop=(dt == NDT - 1))
                if dt == NDT - 1 and ch == NCH - 1:
                    gv = P.inc("tensor", bld, "t_g", 1)
                else:
                    P.raw("tensor", bld)
        wlast[id(wg_b[bi])] = ("t_g", gv)
        P.wait("scalar", "t_g", gv)
        sgv = {}
        stv = {}
        h_base = P.val("s_h")
        for ch in range(2):
            sgv[ch] = P.inc("scalar", lambda E, ch=ch: E.activation(
                out=sgb[ch % 2][:, :], in_=pb[4 + ch].ap()[:, :],
                func=AF.Silu, scale=c0(2), bias=c0(0)), "s_sg", 1)
        P.wait("tensor", "d_w", wuv)
        P.wait("tensor", "s_h", h_base)
        uv = {}
        for ch in range(NCH):
            for dt in range(NDT):
                bld = lambda E, dt=dt, ch=ch, bi=bi: E.matmul(
                    pb[ch].ap()[:, :], wu_b[bi][:, dt, :],
                    xn[dt][:, ch * 512:(ch + 1) * 512],
                    start=(dt == 0), stop=(dt == NDT - 1))
                if dt == NDT - 1:
                    uv[ch] = P.inc("tensor", bld, "t_u", 1)
                else:
                    P.raw("tensor", bld)
        wlast[id(wu_b[bi])] = ("t_u", uv[NCH - 1])
        for ch in range(NCH):
            if ch >= 2:
                P.wait("scalar", "s_h", stv[ch - 2])
                sgv[ch] = P.inc("scalar", lambda E, ch=ch: E.activation(
                    out=sgb[ch % 2][:, :], in_=pb[4 + ch].ap()[:, :],
                    func=AF.Silu, scale=c0(2), bias=c0(0)), "s_sg", 1)
            P.wait("vector", "s_sg", sgv[ch])
            P.wait("vector", "t_u", uv[ch])
            stv[ch] = P.inc("vector", lambda E, ht=ht, ch=ch:
                            E.scalar_tensor_tensor(
                hdn[ht][:, ch * 512:(ch + 1) * 512], sgb[ch % 2][:, :], INV2,
                pb[ch].ap()[:, :], ALU.mult, ALU.mult), "s_h", 1)

    def ffn_ht(ht, pre=None):
        bi = ht % 2
        if pre is None:
            wgv = wload(w_g, ht, wg_b[bi])
            wuv = wload(w_u, ht, wu_b[bi])
        else:
            wgv, wuv = pre
        P.wait("tensor", "d_w", wgv)
        P.wait("tensor", "s_h", P.val("s_h"))
        for ch in range(NCH):
            for dt in range(NDT):
                bld = lambda E, dt=dt, ch=ch, bi=bi: E.matmul(
                    pb[4 + ch].ap()[:, :], wg_b[bi][:, dt, :],
                    xn[dt][:, ch * 512:(ch + 1) * 512],
                    start=(dt == 0), stop=(dt == NDT - 1))
                if dt == NDT - 1 and ch == NCH - 1:
                    gv = P.inc("tensor", bld, "t_g", 1)
                else:
                    P.raw("tensor", bld)
        wlast[id(wg_b[bi])] = ("t_g", gv)
        P.wait("scalar", "t_g", gv)
        P.wait("scalar", "s_h", P.val("s_h"))
        P.wait("tensor", "d_w", wuv)
        for half in range(2):
            chs = (2 * half, 2 * half + 1)
            sgv = {}
            if half == 1:
                P.wait("scalar", "s_h", P.val("s_h"))
            for ch in chs:
                sgv[ch] = P.inc("scalar", lambda E, ch=ch: E.activation(
                    out=sgb[ch % 2][:, :], in_=pb[4 + ch].ap()[:, :],
                    func=AF.Silu, scale=c0(2), bias=c0(0)), "s_sg", 1)
            P.wait("tensor", "s_sg", sgv[chs[1]])
            for ch in chs:
                for dt in range(NDT):
                    bld = lambda E, dt=dt, ch=ch, bi=bi: E.matmul(
                        pb[4 + ch].ap()[:, :], wu_b[bi][:, dt, :],
                        xn[dt][:, ch * 512:(ch + 1) * 512],
                        start=(dt == 0), stop=(dt == NDT - 1))
                    if dt == NDT - 1 and ch == chs[1]:
                        uv = P.inc("tensor", bld, "t_u", 1)
                    else:
                        P.raw("tensor", bld)
            wlast[id(wu_b[bi])] = ("t_u", uv)
            P.wait("scalar", "t_u", uv)
            for ch in chs:
                ue = P.inc("scalar", lambda E, ch=ch: E.activation(
                    out=ubf[ch % 2][:, :], in_=pb[4 + ch].ap()[:, :],
                    func=AF.Copy, scale=INV2, bias=0.0), "s_ue", 1)
                P.wait("gpsimd", "s_ue", ue)
                P.wait("gpsimd", "s_sg", sgv[ch])
                P.inc("gpsimd", lambda E, ht=ht, ch=ch: E.tensor_mul(
                    hdn[ht][:, ch * 512:(ch + 1) * 512], sgb[ch % 2][:, :],
                    ubf[ch % 2][:, :]), "s_h", 1)

    HT_SCHED = []
    _c = [3, 2, 2, 2, 2, 2, 1, 1]
    _ht = 0
    for j in range(NJ):
        HT_SCHED.append(list(range(_ht, _ht + _c[j])))
        _ht += _c[j]

    ks_base = [0]

    _scnt = [0]

    def sum_side(j, pairs, tile_base):
        for side, p in pairs:
            tb_ = tkr if side == "k" else tqr
            P.wait("tensor", "s_tkp", tile_base[0] + p)
            pp = _scnt[0] % 2
            _scnt[0] += 1
            if main:
                P.wait("scalar", "d_scr", P.val("d_scr") - 16)
            for ch in range(NCH):
                bank = ch % 2
                if p == 1 and main:
                    src = (xkf if side == "k" else xqf)[
                        :, ch * 512:(ch + 1) * 512]
                else:
                    src = tb_[:, (p - 1) % 2, ch * 512:(ch + 1) * 512]
                P.wait("tensor", "s_kse", P.val("s_kse") - 1)
                kv_ = P.inc("tensor", lambda E, src=src, bank=bank: E.matmul(
                    pb[bank].ap()[0:2, :], scb[:, 1:3], src,
                    start=True, stop=True), "t_ks", 1)
                P.wait("scalar", "t_ks", kv_)
                if main:
                    P.inc("scalar", lambda E, pp=pp, ch=ch, bank=bank:
                          E.copy(out=kqst[pp][:, ch * 512:(ch + 1) * 512],
                                 in_=pb[bank].ap()[0:2, :]), "s_kse", 1)
                else:
                    P.inc("scalar", lambda E, c=bcell_s: E.copy(
                        out=c, in_=c), "s_kse", 1)
            if main:
                scr = kscr if side == "k" else qscr
                P.inc("scalar", lambda E, scr=scr, j=j, p=p, pp=pp:
                      E.dma_start(out=scr.ap()[2 * j:2 * j + 2, p - 1, :],
                                  in_=kqst[pp][:, :]), "d_scr", 16)

    for j in range(NJ):
        pre_w = {}
        kt = (8 + j) if main else j
        vt = (16 + j) if main else (8 + j)
        wsrc = w_qkv if main else w_kv
        wvk = wload(wsrc, kt, wq[0])
        ekv = proj(wq[0], wvk, xkf, 0.125, (0, 1, 2, 3))
        P.wait("vector", "s_ev", ekv)
        P.wait("vector", "s_sq", P.val("s_sq"))
        ck = P.inc("vector", lambda E: E.tensor_scalar(
            xkf[:, :], xkf[:, :], CLIP_HI, -CLIP_HI, ALU.min, ALU.max),
            "s_ck", 1)
        P.wait("scalar", "s_ck", ck)
        P.inc("scalar", lambda E: E.activation(
            out=ykf[:, :], in_=xkf[:, :], func=AF.Square, scale=c0(1),
            bias=c0(0)), "s_sq", 1)
        wvv = wload(wsrc, vt, wq[1])
        evv = proj(wq[1], wvv, bv, None, (0, 1, 2, 3))
        if main:
            wvq = wload(wsrc, j, wq[0])
            eqv = proj(wq[0], wvq, xqf, 0.125, (0, 1, 2, 3))
            P.wait("vector", "s_ev", eqv)
            cq = P.inc("vector", lambda E: E.tensor_scalar(
                xqf[:, :], xqf[:, :], CLIP_HI, -CLIP_HI, ALU.min, ALU.max),
                "s_ck", 1)
            P.wait("scalar", "s_ck", cq)
            P.inc("scalar", lambda E: E.activation(
                out=yqf[:, :], in_=xqf[:, :], func=AF.Square, scale=c0(1),
                bias=c0(0)), "s_sq", 1)
        P.wait("vector", "s_sq", P.val("s_sq"))
        P.wait("vector", "s_ev", evv)
        if fused:
            P.guard_begin(("sync",), {"sync": regs_npar["sync"]})
            poll_fn(shSSf.ap()[0:1, j:j + 1])
            gv_ = P.inc("sync", lambda E, j=j: E.dma_start(
                out=carry_src[:, j * NP:(j + 1) * NP],
                in_=shSS.ap()[:, j * NP:(j + 1) * NP]), "d_codd", 16)
            P.wait("sync", "d_codd", gv_)
            P.inc("sync", lambda E: E.nop(), "c_rdy", 1)
            P.guard_end()
            P.wait("vector", "c_rdy", P.val("c_rdy"))
            P.wait("vector", "d_ss", P.val("d_ss"))
        if main and not fused:
            P.wait("vector", "s_carry", CARRY_V)
            P.wait("vector", "d_rel", KSC_DONE)
        ks_tile_base = P.val("s_tkp")
        tbh = [ks_tile_base]
        if main:
            for p in range(1, 6):
                if p >= 2:
                    P.wait("vector", "t_ks", ks_base[0] + 8 * (p - 2))
                if p == 1:
                    ksrc = xkf[:, :]
                else:
                    ktk = tkr[:, (p - 1) % 2, :]
                    cheb(ykf, xkf, p, ktk, ctv, None)
                    ksrc = ktk
                P.raw("vector", lambda E, s=ksrc: E.scalar_tensor_tensor(
                    ctv[:, :], s, 1.0, bv[:, :], ALU.mult, ALU.mult))
                init = carry_kv[:, j * NP + (p - 1):j * NP + p]
                P.raw("vector", lambda E, init=init: E.tensor_tensor_scan(
                    ckvp[:, :], ctv[:, :], ctv[:, :], init, ALU.add,
                    ALU.bypass))
                P.raw("vector", lambda E, p=p: E.tensor_copy(
                    sstate[:, p - 1:p], ckvp[:, S - 1:S]))
                if p == 1:
                    P.raw("vector", lambda E, j=j: E.scalar_tensor_tensor(
                        num[j][:, :], xqf[:, :], BETA[0], ckvp[:, :], ALU.mult,
                        ALU.mult))
                else:
                    qtk = tqr[:, (p - 1) % 2, :]
                    cheb(yqf, xqf, p, qtk, ctv, None)
                    P.raw("vector", lambda E, s=qtk, p=p:
                          E.scalar_tensor_tensor(
                        ctv[:, :], s, BETA[p - 1], ckvp[:, :], ALU.mult,
                        ALU.mult))
                    P.raw("vector", lambda E, j=j: E.tensor_add(
                        num[j][:, :], num[j][:, :], ctv[:, :]))
                P.inc("vector", lambda E: E.tensor_copy(bcell_v, bcell_v),
                      "s_tkp", 1)
        else:
            for p in range(1, 6):
                if p >= 3:
                    P.wait("vector", "t_ks", ks_base[0] + 4 * (p - 2))
                for ch in range(NCH):
                    c5 = slice(ch * 512, (ch + 1) * 512)
                    ktk = tkr[:, (p - 1) % 2, c5]
                    if p == 1:
                        P.raw("vector", lambda E, s=ktk, c5=c5: E.tensor_copy(
                            s, xkf[:, c5]))
                    else:
                        cheb(ykf, xkf, p, ktk, ctmp, ch)
                    P.raw("vector", lambda E, s=ktk, c5=c5, p=p:
                          E.scalar_tensor_tensor(
                        tqr[:, (p - 1) % 2, c5], s, 1.0, bv[:, c5], ALU.mult,
                        ALU.mult))
                jp = j * NP + (p - 1)
                P.raw("vector", lambda E, jp=jp, p=p: E.tensor_reduce(
                    tot_kv[:, jp:jp + 1], tqr[:, (p - 1) % 2, :],
                    mybir.AxisListType.X, ALU.add))
                P.raw("vector", lambda E, p=p: E.tensor_reduce(
                    colT[:, p:p + 1], tkr[:, (p - 1) % 2, :],
                    mybir.AxisListType.X, ALU.add))
                P.inc("vector", lambda E: E.tensor_copy(bcell_v, bcell_v),
                      "s_tkp", 1)
        if main:
            hts = HT_SCHED[j]
            if len(hts) >= 1:
                ffn_ht(hts[0], pre_w.get(hts[0]))
            sum_side(j, [("k", 1), ("q", 1), ("k", 2), ("q", 2)], tbh)
            if len(hts) >= 2:
                ffn_ht(hts[1], pre_w.get(hts[1]))
            sum_side(j, [("k", 3), ("q", 3), ("k", 4), ("q", 4)], tbh)
            if len(hts) >= 3:
                ffn_ht(hts[2])
            sum_side(j, [("k", 5), ("q", 5)], tbh)
        else:
            sum_side(j, [("k", p) for p in range(1, 6)], tbh)
        ks_base[0] = P.val("t_ks")
        if main:
            tkc_now = P.val("s_tkp")
            if fused:
                P.guard_begin(("sync",), {"sync": regs_par["sync"]})
                P.wait("sync", "s_tkp", tkc_now)
                sv_ = P.inc("sync", lambda E, j=j: E.dma_start(
                    out=shSS.ap()[:, j * NP:(j + 1) * NP], in_=sstate[:, :]),
                    "d_ss", 16)
                P.wait("sync", "d_ss", sv_)
                P.inc("sync", lambda E, j=j: E.dma_start(
                    out=shSSf.ap()[0:1, j:j + 1], in_=stok32[:, :]), "d_ss", 16)
                P.guard_end()
        if not main:
            tv_ = P.inc("vector", lambda E: E.tensor_copy(bcell_v, bcell_v),
                        "s_colT", 1)
            P.wait("scalar", "s_colT", tv_)
            cb = P.inc("scalar", lambda E: E.copy(out=colTb[:, 1:6],
                                                  in_=colT[:, 1:6]), "s_ctb", 1)
            P.wait("tensor", "s_ctb", cb)
            P.wait("tensor", "s_kse", P.val("s_kse"))
            ksm = P.inc("tensor", lambda E, j=j: E.matmul(
                pb[2].ap()[0:2, 0:5], scb[:, 1:3], colTb[:, 1:6],
                start=True, stop=True), "t_kst", 1)
            P.wait("scalar", "t_kst", ksm)
            P.inc("scalar", lambda E, j=j: E.copy(
                out=kst2[:, j * NP:(j + 1) * NP], in_=pb[2].ap()[0:2, 0:5]),
                "s_kst", 1)
        if stop_after is not None:
            P.barrier(bcell_v, bcell_s)
        else:
            b4 = P.val("bar4") + 4
            P.inc("tensor", lambda E: E.nop(), "bar4", 1)
            P.inc("gpsimd", lambda E: E.engine_nop(), "bar4", 1)
            P.inc("vector", lambda E: E.tensor_copy(bcell_v, bcell_v),
                  "bar4", 1)
            P.inc("scalar", lambda E: E.copy(out=bcell_s, in_=bcell_s),
                  "bar4", 1)
            for e in ("tensor", "scalar", "vector", "gpsimd"):
                P.wait(e, "bar4", b4)
        if stop_after == f"tile{j}":
            return _finish()

    if not main:
        tf = P.inc("vector", lambda E: E.tensor_copy(bcell_v, bcell_v), "s_fin", 1)
        P.wait("sync", "s_fin", tf)
        P.wait("sync", "s_kst", P.val("s_kst"))
        P.inc("sync", lambda E: E.dma_start(out=o_tot[:, :], in_=tot_kv[:, :]),
              "d_out", 16)
        P.inc("sync", lambda E: E.dma_start(out=o_kst[:, :], in_=kst2[:, :]),
              "d_out", 16)
        P.wait("sync", "d_out", P.val("d_out"))
        cxB.close()
        cxH.close()
        with nc.Block() as block:
            P.realize(block)
        return nc

    cxB.close()
    cxD = ExitStack()
    ks16 = sb(cxD, "ks16", [16, NP, S], BF16)
    qs16 = sb(cxD, "qs16", [16, NP, S], BF16)
    kspbuf = sb(cxD, "kspbuf", [16, S], BF16)
    denb = sb(cxD, "denb", [16, S], BF16)
    rdb = sb(cxD, "rdb", [16, S], BF16)
    rdbc = sb(cxD, "rdbc", [128, 512], BF16)
    P.wait("sync", "d_scr", P.val("d_scr"))
    k16v = ld(ks16[:, :, :], kscr.ap()[:, :, :], "d_k16")
    q16v = ld(qs16[:, :, :], qscr.ap()[:, :, :], "d_k16")
    P.wait("vector", "d_k16", q16v)
    P.wait("scalar", "d_k16", q16v)
    if fused:
        P.guard_begin(("sync",), {"sync": regs_npar["sync"]})
        poll_fn(shKSf.ap()[0:1, 0:1])
        kv_ = P.inc("sync", lambda E: E.dma_start(
            out=ksc16[:, :], in_=shKS.ap()[:, :]), "d_kso", 16)
        P.wait("sync", "d_kso", kv_)
        P.inc("sync", lambda E: E.nop(), "c_ks", 1)
        P.guard_end()
        P.wait("vector", "c_ks", P.val("c_ks"))
    for p in range(1, 6):
        P.raw("vector", lambda E, p=p: E.tensor_tensor_scan(
            kspbuf[:, :],
            ks16[:, p - 1:p, :].rearrange("h one s -> h (one s)"),
            ks16[:, p - 1:p, :].rearrange("h one s -> h (one s)"),
            ksc16[:, p - 1:p], ALU.add, ALU.bypass))
        if fused:
            sv_f = P.inc("vector", lambda E: E.tensor_copy(bcell_v, bcell_v),
                         "c_sf", 1)
            P.wait("scalar", "c_sf", sv_f)
            sc_f = P.inc("scalar", lambda E, p=p: E.copy(
                out=ksfin[:, p - 1:p], in_=kspbuf[:, S - 1:S]), "s_sf", 1)
            P.wait("vector", "s_sf", sc_f)
        P.raw("vector", lambda E, p=p: E.scalar_tensor_tensor(
            kspbuf[:, :], kspbuf[:, :], BETA[p - 1],
            qs16[:, p - 1:p, :].rearrange("h one s -> h (one s)"), ALU.mult,
            ALU.mult))
        if p == 1:
            P.inc("vector", lambda E: E.tensor_copy(denb[:, :], kspbuf[:, :]),
                  "s_dka", 1)
        else:
            P.inc("vector", lambda E: E.tensor_add(denb[:, :], denb[:, :],
                                                   kspbuf[:, :]), "s_dka", 1)
    if fused:
        kfv = P.inc("vector", lambda E: E.tensor_copy(bcell_v, bcell_v),
                    "c_kf", 1)
        P.guard_begin(("sync",), {"sync": regs_par["sync"]})
        P.wait("sync", "c_kf", kfv)
        pv_ = P.inc("sync", lambda E: E.dma_start(
            out=shKS.ap()[:, :], in_=ksfin[:, :]), "d_ksp", 16)
        P.wait("sync", "d_ksp", pv_)
        P.inc("sync", lambda E: E.dma_start(
            out=shKSf.ap()[0:1, 0:1], in_=stok32[:, :]), "d_ksp", 16)
        P.guard_end()
    P.raw("vector", lambda E: E.tensor_scalar(denb[:, :], denb[:, :], EPS_DEN,
                                              None, ALU.add))
    def _recip(E):
        with nc.allow_low_precision(reason="attn denominator: bf16 suffices"):
            return E.reciprocal(rdb[:, :], denb[:, :])
    P.wait("tensor", "s_h", P.val("s_h"))
    rb = P.inc("vector", _recip, "s_rdb", 1)
    if os.environ.get("KDBG"):
        o_dbg3 = nc.dram_tensor("dbg3", [4, 16, S], BF16, kind="ExternalOutput")
        P.wait("gpsimd", "s_rdb", rb)
        _d3 = [denb[:, :],
               ks16[:, 0:1, :].rearrange("h one s -> h (one s)"),
               qs16[:, 0:1, :].rearrange("h one s -> h (one s)"),
               rdb[:, :]]
        for di, ap_ in enumerate(_d3):
            P.inc("gpsimd", lambda E, di=di, ap_=ap_: E.dma_start(
                out=o_dbg3.ap()[di], in_=ap_), "d_dbg3", 16)
        o_dbg4 = nc.dram_tensor("dbg4", [2, 16, NP], F32, kind="ExternalOutput")
        P.inc("gpsimd", lambda E: E.dma_start(
            out=o_dbg4.ap()[0], in_=ksc16[:, :]), "d_dbg3", 16)
        P.inc("gpsimd", lambda E: E.dma_start(
            out=o_dbg4.ap()[1], in_=ksfin[:, :]), "d_dbg3", 16)
        P.wait("gpsimd", "d_dbg3", P.val("d_dbg3"))
    P.wait("tensor", "s_rdb", rb)
    for j in range(NJ):
        for ch in range(NCH):
            bank = 6 + (ch % 2)
            P.wait("tensor", "s_rde", P.val("s_rde") - 1)
            rm = P.inc("tensor", lambda E, j=j, ch=ch, bank=bank: E.matmul(
                pb[bank].ap()[:, :], sel_sb[:, j * 128:(j + 1) * 128],
                rdb[:, ch * 512:(ch + 1) * 512], start=True, stop=True), "t_rd", 1)
            P.wait("scalar", "t_rd", rm)
            P.wait("scalar", "s_oh", P.val("s_oh") - 1)
            re = P.inc("scalar", lambda E, bank=bank: E.copy(
                out=rdbc[:, :], in_=pb[bank].ap()[:, :]), "s_rde", 1)
            P.wait("vector", "s_rde", re)
            P.inc("vector", lambda E, j=j, ch=ch: E.tensor_mul(
                num[j][:, ch * 512:(ch + 1) * 512],
                num[j][:, ch * 512:(ch + 1) * 512], rdbc[:, :]), "s_oh", 1)
    P.barrier(bcell_v, bcell_s)
    cxD.close()
    if stop_after == "den":
        return _finish()

    cxH2 = ExitStack()
    for t in range(NH_IN, NHT):
        hdn.append(sb(cxH2, f"hdn{t}", [128, S], BF16))
    for ht in range(NH_IN, NHT):
        ffn_ht_post(ht)
    P.barrier(bcell_v, bcell_s)
    if stop_after == "ffn":
        return _finish()

    if os.environ.get("KDBG"):
        o_dbg = nc.dram_tensor("dbg", [6, 128, S], BF16, kind="ExternalOutput")
        dv = P.inc("vector", lambda E: E.tensor_copy(bcell_v, bcell_v), "s_dbg", 1)
        P.wait("gpsimd", "s_dbg", dv)
        for di, buf in enumerate([num[0], hdn[0], hdn[17], hdn[23], xn[0]]):
            P.inc("gpsimd", lambda E, di=di, buf=buf: E.dma_start(
                out=o_dbg.ap()[di], in_=buf[:, :]), "d_dbg", 16)
        P.wait("gpsimd", "d_dbg", P.val("d_dbg"))

    cxC2 = ExitStack()
    wdbf = [sb(cxC2, f"wdbf{i}", [128, NHT, 128], BF16) for i in range(2)]
    wobf = [sb(cxC2, f"wobf{i}", [128, NDT, 128], BF16) for i in range(2)]
    xre = [sb(cxC2, f"xre{i}", [128, 512]) for i in range(2)]

    for e2 in range(NDT):
        bi = e2 % 2
        wdv = wload(w_d, e2, wdbf[bi], nt=NHT)
        wov = wload(w_out, e2, wobf[bi])
        P.wait("tensor", "d_w", wov)
        fm = {}
        for ch in range(NCH):
            bank = ch % 2
            P.wait("sync", "s_y", P.val("s_y") - 1)
            xv = P.inc("sync", lambda E, e2=e2, ch=ch: E.dma_start(
                out=xre[ch % 2][:, :],
                in_=xT.ap()[e2 * 128:(e2 + 1) * 128, ch * 512:(ch + 1) * 512]),
                "d_xr", 16)
            P.wait("tensor", "s_y", P.val("s_y") - 1)
            for t in range(NHT):
                P.raw("tensor", lambda E, t=t, ch=ch, bank=bank, bi=bi: E.matmul(
                    pb[bank].ap()[:, :], wdbf[bi][:, t, :],
                    hdn[t][:, ch * 512:(ch + 1) * 512],
                    start=(t == 0), stop=False))
            for dt in range(NDT):
                bld = lambda E, dt=dt, ch=ch, bank=bank, bi=bi: E.matmul(
                    pb[bank].ap()[:, :], wobf[bi][:, dt, :],
                    num[dt][:, ch * 512:(ch + 1) * 512],
                    start=False, stop=(dt == NDT - 1))
                if dt == NDT - 1:
                    fm[ch] = P.inc("tensor", bld, "t_f", 1)
                else:
                    P.raw("tensor", bld)
            P.wait("vector", "t_f", fm[ch])
            P.wait("vector", "d_xr", xv)
            P.wait("vector", "d_y", P.val("d_y") - 1)
            yv = P.inc("vector", lambda E, ch=ch, bank=bank: E.tensor_add(
                xre[ch % 2][:, :], xre[ch % 2][:, :], pb[bank].ap()[:, :]),
                "s_y", 1)
            P.wait("sync", "s_y", yv)
            P.inc("sync", lambda E, e2=e2, ch=ch: E.dma_start(
                out=o_y.ap()[e2 * 128:(e2 + 1) * 128, ch * 512:(ch + 1) * 512],
                in_=xre[ch % 2][:, :]), "d_y", 16)
        wlast[id(wdbf[bi])] = ("t_f", fm[NCH - 1])
        wlast[id(wobf[bi])] = ("t_f", fm[NCH - 1])
        if stop_after == f"fin{e2}":
            P.wait("sync", "d_y", P.val("d_y"))
            return _finish()
    P.wait("sync", "d_y", P.val("d_y"))
    cxC2.close()
    cxH2.close()
    cxH.close()

    with nc.Block() as block:
        P.realize(block)
    return nc


_programs = {}
_token = int.from_bytes(os.urandom(3), "little") + 7

def _get_program(mode):
    if mode not in _programs:
        _programs[mode] = build_program(mode)
    return _programs[mode]


def _host_consts():
    consts = np.zeros((128, 8), np.float32)
    consts[:, 1] = 1.0
    consts[:, 2] = INV127
    cbf = np.zeros((128, 4), ml_dtypes.bfloat16)
    cbf[:, 0] = 1.0
    cbf[:64, 1] = 1.0
    cbf[64:, 2] = 1.0
    onesrow = np.ones((1, 128), np.float32)
    sel = np.zeros((16, NJ * 128), ml_dtypes.bfloat16)
    for j in range(NJ):
        sel[2 * j, j * 128:j * 128 + 64] = 1.0
        sel[2 * j + 1, j * 128 + 64:(j + 1) * 128] = 1.0
    return consts, cbf, onesrow, sel


def _tiles_t(w):
    E = w.shape[0]
    return np.ascontiguousarray(
        w.T.reshape(w.shape[1], E // 128, 128).transpose(1, 0, 2))


def kernel(L0_attn_x, L0_attn_W_in, L0_attn_W_out, L0_ffn_w_gate, L0_ffn_w_up,
           L0_ffn_w_down):
    x = np.ascontiguousarray(np.asarray(L0_attn_x, np.float32).reshape(16384, D))
    W_in = np.asarray(L0_attn_W_in, np.float32)
    W_out = np.asarray(L0_attn_W_out, np.float32)
    wg = np.asarray(L0_ffn_w_gate, np.int32)
    wu = np.asarray(L0_ffn_w_up, np.int32)
    wd = np.asarray(L0_ffn_w_down, np.int32)

    consts, cbf, onesrow, sel = _host_consts()
    w_qkv = _tiles_t(W_in)
    w_kv = np.ascontiguousarray(w_qkv[8:24])
    w_out_t = _tiles_t(W_out)
    w_g = _tiles_t(wg)
    w_u = _tiles_t(wu)
    w_d = _tiles_t(wd)
    xT_chunks = [np.ascontiguousarray(x[c * S:(c + 1) * S, :].T)
                 for c in range(NCORES)]

    global _token
    _token += 1
    tok = _token
    nc2 = _get_program("fused")
    in2 = []
    for c in range(NCORES):
        pair = c // 2
        if c % 2 == 0:
            r0, r1 = 0, 2048 * pair
        else:
            r0, r1 = 2048 * pair, 4096 * pair
        ntiles = (r1 - r0) // 512
        xr = np.zeros((D, 12 * 512), np.float32)
        if ntiles > 0:
            xr[:, :r1 - r0] = x[r0:r1, :].T
        rp = np.zeros((1, 8), np.int32)
        rp[0, 0] = ntiles
        rp[0, 1] = 1 - (c % 2)
        rp[0, 2] = tok
        in2.append({"xT": xT_chunks[c], "xRT": np.ascontiguousarray(xr),
                    "rparam": rp, "consts": consts, "cbf": cbf,
                    "onesrow": onesrow, "sel": sel, "w_qkv": w_qkv,
                    "w_out": w_out_t, "w_g": w_g, "w_u": w_u, "w_d": w_d})
    res2 = run_bass_kernel_spmd(nc2, in2, core_ids=list(range(NCORES)))
    y = np.concatenate([res2.results[c]["yT"].T for c in range(NCORES)], axis=0)
    return np.ascontiguousarray(y).reshape(1, 16384, D)


# version 18
# speedup vs baseline: 1.9778x; 1.9778x over previous
import os
import numpy as np
import ml_dtypes
import concourse.bass as bass
import concourse.mybir as mybir
from concourse.bass_utils import run_bass_kernel_spmd
from contextlib import ExitStack

F32 = mybir.dt.float32
BF16 = mybir.dt.bfloat16
I32 = mybir.dt.int32
AF = mybir.ActivationFunctionType
ALU = mybir.AluOpType

NCORES = 8
D = 1024
S = 2048
NDT = D // 128
NCH = S // 512
HID = 3072
NHT = HID // 128
NP = 5
NJ = 8

CLIP_HI = float(np.float32(1.0) - np.float32(1e-6))
LN_EPS = 1e-5
EPS_DEN = 1e-7
INV127 = float(np.float32(1.0) / np.float32(127.0))


def _beta():
    j = np.arange(6, dtype=np.float32)
    alpha = (j + 1.0) ** (-1.5)
    tail = np.flip(np.cumsum(np.flip(alpha)))
    beta = np.concatenate([np.zeros(1, np.float32), tail[1:].astype(np.float32),
                           np.zeros(5, np.float32)])
    beta = beta / beta.sum()
    return [float(beta[p]) for p in range(1, 6)]

BETA = _beta()


class Plan:
    ENGS = ("sync", "scalar", "vector", "tensor", "gpsimd")

    def __init__(self, nc):
        self.nc = nc
        self.ops = {e: [] for e in self.ENGS}
        self.sems = {}
        self.barn = 0
        self.region = None
        self.ifstack = {e: [] for e in self.ENGS}

    def sem(self, name):
        if name not in self.sems:
            h = self.nc.ctx.enter_context(self.nc.semaphore(name))
            self.sems[name] = [h, 0]
        return self.sems[name]

    def raw(self, eng, build):
        self.ops[eng].append(lambda E, b=build: b(E))

    def inc(self, eng, build, sem, k=1):
        s = self.sem(sem)
        s[1] += k
        if self.region is not None:
            r = self.region.setdefault(eng, {})
            r[sem] = r.get(sem, 0) + k
        self.ops[eng].append(lambda E, b=build, h=s[0], kk=k: b(E).then_inc(h, kk))
        return s[1]

    def guard_begin(self, engs, regs):
        assert self.region is None
        self.region = {}
        self._guard_engs = list(engs)
        for e in engs:
            st = self.ifstack[e]
            def open_(E, r=regs[e], st=st):
                g = E.If(r)
                g.__enter__()
                st.append(g)
            self.ops[e].append(open_)

    def guard_end(self):
        region = self.region
        self.region = None
        for e in self._guard_engs:
            st = self.ifstack[e]
            incs = region.get(e, {})
            def close(E, st=st, incs=dict(incs)):
                st.pop().__exit__(None, None, None)
                if incs:
                    eg = E.Else()
                    eg.__enter__()
                    for sem, k in incs.items():
                        E.sem_inc(self.sems[sem][0], k)
                    eg.__exit__(None, None, None)
            self.ops[e].append(close)
        return region

    def wait(self, eng, sem, val):
        if val <= 0:
            return
        s = self.sem(sem)
        self.ops[eng].append(lambda E, h=s[0], v=val: E.wait_ge(h, v))

    def val(self, sem):
        return self.sem(sem)[1]

    def barrier(self, bcell_v, bcell_s):
        self.barn += 1
        tgt = 5 * self.barn
        s = self.sem("bar")
        self.ops["tensor"].append(lambda E, h=s[0]: E.nop().then_inc(h, 1))
        self.ops["sync"].append(lambda E, h=s[0]: E.nop().then_inc(h, 1))
        self.ops["gpsimd"].append(lambda E, h=s[0]: E.engine_nop().then_inc(h, 1))
        self.ops["vector"].append(
            lambda E, h=s[0], c=bcell_v: E.tensor_copy(c, c).then_inc(h, 1))
        self.ops["scalar"].append(
            lambda E, h=s[0], c=bcell_s: E.copy(out=c, in_=c).then_inc(h, 1))
        s[1] = tgt
        for e in self.ENGS:
            self.wait(e, "bar", tgt)

    def realize(self, block):
        for eng in self.ENGS:
            def run(E, ops=self.ops[eng]):
                for f in ops:
                    f(E)
            getattr(block, eng)(run)


def build_program(mode, stop_after=None):
    fused = mode == "fused"
    main = mode in ("main", "fused")
    nc = bass.Bass("TRN2", target_bir_lowering=False, detect_race_conditions=False, dynamic_dma_scratch_size=16384)

    def din(name, shape, dt=F32):
        return nc.dram_tensor(name, shape, dt, kind="ExternalInput")

    xT = din("xT", [D, S])
    consts = din("consts", [128, 8])
    cbf = din("cbf", [128, 4], BF16)
    onesrow = din("onesrow", [1, 128])
    if main:
        w_qkv = din("w_qkv", [24, D, 128])
        w_out = din("w_out", [NDT, D, 128])
        w_g = din("w_g", [NHT, D, 128], I32)
        w_u = din("w_u", [NHT, D, 128], I32)
        w_d = din("w_d", [NDT, HID, 128], I32)
        sel = din("sel", [16, NJ * 128], BF16)
        if not fused:
            all_tot = din("all_tot", [NCORES * 128, 40])
            all_ks = din("all_ks", [NCORES * 2, 40])
            mask = din("mask", [128, NCORES])
            ksmask = din("ksmask", [2, NCORES])
        else:
            xRT = din("xRT", [D, 12 * 512])
            rparam = din("rparam", [1, 8], I32)
            shR_kv = [nc.dram_tensor(f"shR_kv{i}", [128, 40], F32,
                                     kind="Internal", addr_space="Shared")
                      for i in range(2)]
            shR_ks = [nc.dram_tensor(f"shR_ks{i}", [2, 40], F32,
                                     kind="Internal", addr_space="Shared")
                      for i in range(2)]
            shRf = nc.dram_tensor("shRf", [1, 2], I32, kind="Internal",
                                  addr_space="Shared")
            shSS = nc.dram_tensor("shSS", [128, 40], F32, kind="Internal",
                                  addr_space="Shared")
            shSSf = nc.dram_tensor("shSSf", [1, NJ], I32, kind="Internal",
                                   addr_space="Shared")
            shKS = nc.dram_tensor("shKS", [16, NP], F32, kind="Internal",
                                  addr_space="Shared")
            shKSf = nc.dram_tensor("shKSf", [1, 1], I32, kind="Internal",
                                   addr_space="Shared")
        o_y = nc.dram_tensor("yT", [D, S], F32, kind="ExternalOutput")
        kscr = nc.dram_tensor("kscr", [16, NP, S], BF16, kind="Internal")
        qscr = nc.dram_tensor("qscr", [16, NP, S], BF16, kind="Internal")
    else:
        w_kv = din("w_kv", [16, D, 128])
        o_tot = nc.dram_tensor("tot", [128, 40], F32, kind="ExternalOutput")
        o_kst = nc.dram_tensor("kst", [2, 40], F32, kind="ExternalOutput")

    sb = lambda cx, name, shape, dt=F32: cx.enter_context(
        nc.sbuf_tensor(name, shape, dt))
    P = Plan(nc)

    cx0 = nc.ctx
    sc = sb(cx0, "sc", [128, 8])
    scb = sb(cx0, "scb", [128, 4], BF16)
    sonesrow = sb(cx0, "sonesrow", [1, 128])
    xn = [sb(cx0, f"xn{i}", [128, S], BF16) for i in range(NDT)]
    wq = [sb(cx0, f"wq{i}", [128, NDT, 128], BF16) for i in range(2)]
    tot_kv = sb(cx0, "tot_kv", [128, 40])
    bcv = sb(cx0, "bcv", [1, 1])
    bcs = sb(cx0, "bcs", [1, 1])
    if main:
        num = [sb(cx0, f"num{j}", [128, S], BF16) for j in range(NJ)]
        ksc16 = sb(cx0, "ksc16", [16, NP])
        sel_sb = sb(cx0, "sel_sb", [16, NJ * 128], BF16)
        if not fused:
            stot = sb(cx0, "stot", [128, NCORES, 40])
            sks = sb(cx0, "sks", [2, NCORES, 40])
            smask = sb(cx0, "smask", [128, NCORES])
            sksmask = sb(cx0, "sksmask", [2, NCORES])
            cacc = [sb(cx0, f"cacc{i}", [128, 40]) for i in range(2)]
            kacc = [sb(cx0, f"kacc{i}", [2, 40]) for i in range(2)]
        else:
            carry_src = sb(cx0, "carry_src", [128, 40])
            totR = sb(cx0, "totR", [128, 40])
            colR = sb(cx0, "colR", [128, 40])
            colRb = sb(cx0, "colRb", [128, 40], BF16)
            totc = sb(cx0, "totc", [128, 40])
            colc = sb(cx0, "colc", [128, 40])
            kstR = sb(cx0, "kstR", [2, 40])
            kstR2 = sb(cx0, "kstR2", [2, 40])
            tmpkv = sb(cx0, "tmpkv", [128, 40])
            tmpks = sb(cx0, "tmpks", [2, 40])
            ksfin = sb(cx0, "ksfin", [16, NP])
            sparam = sb(cx0, "sparam", [1, 8], I32)
            spoll = sb(cx0, "spoll", [1, 1], I32)
            stok32 = sb(cx0, "stok32", [1, 1], I32)
    else:
        colT = sb(cx0, "colT", [128, 8])
        colTb = sb(cx0, "colTb", [128, 8], BF16)
        kst2 = sb(cx0, "kst2", [2, 40])
        totx = sb(cx0, "totx", [128, 4])

    pb = [nc.ctx.enter_context(nc.psum_tensor(f"pb{i}", [128, 512], F32))
          for i in range(8)]

    bcell_v = bcv[0:1, 0:1]
    bcell_s = bcs[0:1, 0:1]
    c0 = lambda i, p=128: sc[0:p, i:i + 1]

    def ld(dst, src, sem="d_misc"):
        return P.inc("sync", lambda E, d=dst, s=src: E.dma_start(out=d, in_=s),
                     sem, 16)

    ld(sc[:, :], consts[:, :])
    ld(scb[:, :], cbf[:, :])
    ld(sonesrow[:, :], onesrow[:, :])
    misc_n = 3
    if main:
        ld(sel_sb[:, :], sel[:, :])
        misc_n += 1
        if not fused:
            ld(stot[:, :, :], all_tot.ap().rearrange("(r p) c -> p r c", p=128))
            ld(sks[:, :, :], all_ks.ap().rearrange("(r p) c -> p r c", p=2))
            ld(smask[:, :], mask[:, :])
            ld(sksmask[:, :], ksmask[:, :])
            misc_n += 4
        else:
            ld(sparam[:, :], rparam[:, :])
            misc_n += 1
    P.raw("vector", lambda E: E.memset(bcv[:, :], 0.0))
    P.raw("scalar", lambda E: E.memzero(bcs[:, :]))

    wlast = {}

    def wload(w, t, buf, nt=NDT):
        key = id(buf)
        if key in wlast:
            P.wait("gpsimd", wlast[key][0], wlast[key][1])
        return P.inc("gpsimd", lambda E, w=w, t=t, b=buf, nt=nt: E.dma_start(
            out=b[:, :, :],
            in_=w.ap()[t].rearrange("(dt p) e -> p dt e", p=128)), "d_w", 16)

    if fused:
        GENGS = ("sync", "scalar", "vector", "tensor")
        regs_cnt, regs_g = {}, {}
        regs_par, regs_npar = {}, {}
        eng_of = {"sync": nc.sync, "scalar": nc.scalar, "vector": nc.vector,
                  "tensor": nc.tensor, "gpsimd": nc.gpsimd}
        for e in GENGS:
            regs_cnt[e] = eng_of[e].alloc_register(f"rcnt_{e}")
            regs_g[e] = eng_of[e].alloc_register(f"rg_{e}")
        for e in ("sync", "vector"):
            regs_par[e] = eng_of[e].alloc_register(f"rpar_{e}")
            regs_npar[e] = eng_of[e].alloc_register(f"rnp_{e}")
        r_f = nc.sync.alloc_register("r_f")
        r_go = nc.sync.alloc_register("r_go")
        r_pw = nc.sync.alloc_register("r_pw")
        r_tok = nc.sync.alloc_register("r_tok")

        def poll_fn(flag_ap):
            P.raw("sync", lambda E: E.reg_mov(r_go, 1))
            def loop(E, flag_ap=flag_ap):
                with E.While(r_go):
                    E.dma_start(out=spoll[:, :], in_=flag_ap).then_inc(
                        P.sem("r_pd")[0], 16)
                    E.reg_add(r_pw, r_pw, 16)
                    E.wait_ge(P.sem("r_pd")[0], r_pw)
                    E.reg_load(r_f, spoll[0:1, 0:1])
                    E.reg_alu(r_go, r_f, r_tok, ALU.not_equal)
            P.raw("sync", loop)

    def emit_r_phase_and_exchange():
        NRT = 12
        for e in GENGS:
            P.wait(e, "d_misc", misc_n * 16)
            P.raw(e, lambda E, r=regs_cnt[e]: E.reg_load(r, sparam[0:1, 0:1]))
        for e in ("sync", "vector"):
            P.raw(e, lambda E, r=regs_par[e]: E.reg_load(r, sparam[0:1, 1:2]))
            P.raw(e, lambda E, ri=regs_par[e], ro=regs_npar[e]: E.reg_alu(
                ro, ri, 0, ALU.is_equal))
        P.raw("sync", lambda E: E.reg_load(r_tok, sparam[0:1, 2:3]))
        P.raw("sync", lambda E: E.reg_mov(r_pw, 0))
        poll = poll_fn

        cxR = ExitStack()
        wkv = sb(cxR, "wkv", [128, 16, NDT, 128], BF16)
        sxR = sb(cxR, "sxR", [128, NDT, 512])
        xnR = sb(cxR, "xnR", [128, NDT, 512], BF16)
        rxb = [sb(cxR, f"rxb{i}", [128, 512], BF16) for i in range(2)]
        rx2 = [sb(cxR, f"rx2{i}", [128, 512], BF16) for i in range(2)]
        mR = sb(cxR, "mR", [1, 512]); e2R = sb(cxR, "e2R", [1, 512])
        vR = sb(cxR, "vR", [1, 512]); sR = sb(cxR, "sR", [1, 512])
        rsR = sb(cxR, "rsR", [1, 512])
        ekR = sb(cxR, "ekR", [128, 512], BF16)
        evR = sb(cxR, "evR", [128, 512], BF16)
        bxkR = sb(cxR, "bxkR", [128, 512], BF16)
        ykR = sb(cxR, "ykR", [128, 512], BF16)
        tkR = sb(cxR, "tkR", [128, 512], BF16)
        tmR = sb(cxR, "tmR", [128, 512], BF16)
        tvS = sb(cxR, "tvS", [128, 512], BF16)
        onepl = sb(cxR, "onepl", [128, 512], BF16)
        negpl = sb(cxR, "negpl", [128, 512], BF16)
        tbufR = sb(cxR, "tbufR", [128, 512])

        P.raw("vector", lambda E: E.memset(totR[:, :], 0.0))
        P.raw("vector", lambda E: E.memset(colR[:, :], 0.0))
        P.raw("vector", lambda E: E.memset(kstR[:, :], 0.0))
        P.raw("vector", lambda E: E.memset(totc[:, :], 0.0))
        P.raw("vector", lambda E: E.memset(colc[:, :], 0.0))
        P.raw("vector", lambda E: E.memset(onepl[:, :], 1.0))
        P.raw("vector", lambda E: E.memset(negpl[:, :], -1.0))
        P.raw("vector", lambda E: E.memset(carry_src[:, :], 0.0))
        P.raw("scalar", lambda E: E.copy(out=stok32[:, :], in_=sparam[0:1, 2:3]))
        for t in range(16):
            wkl = P.inc("gpsimd", lambda E, t=t: E.dma_start(
                out=wkv[:, t, :, :],
                in_=w_qkv.ap()[8 + t].rearrange("(dt p) e -> p dt e", p=128)),
                "r_wk", 16)
        P.wait("tensor", "r_wk", wkl)

        for i in range(NRT):
            for e in GENGS:
                P.raw(e, lambda E, r=regs_cnt[e], g=regs_g[e], i=i: E.reg_alu(
                    g, r, i, ALU.is_gt))
            P.guard_begin(GENGS, regs_g)
            xv = 0
            for dt in range(NDT):
                xv = P.inc("sync", lambda E, dt=dt, i=i: E.dma_start(
                    out=sxR[:, dt, :],
                    in_=xRT.ap()[dt * 128:(dt + 1) * 128, i * 512:(i + 1) * 512]),
                    "r_x", 16)
            for dt in range(NDT):
                b = dt % 2
                P.wait("scalar", "r_x", xv)
                if True:
                    P.wait("scalar", "r_smm", P.val("r_smm") - 1)
                P.raw("scalar", lambda E, dt=dt, b=b: E.copy(
                    out=rxb[b][:, :], in_=sxR[:, dt, :]))
                cv = P.inc("scalar", lambda E, dt=dt, b=b: E.activation(
                    out=rx2[b][:, :], in_=sxR[:, dt, :], func=AF.Square,
                    scale=c0(1), bias=c0(0)), "r_cv", 1)
                P.wait("tensor", "r_cv", cv)
                if dt == 0:
                    P.wait("tensor", "r_var", P.val("r_var"))
                P.raw("tensor", lambda E, b=b, dt=dt: E.matmul(
                    pb[0].ap()[0:1, :], scb[:, 0:1], rxb[b][:, :],
                    start=(dt == 0), stop=(dt == NDT - 1)))
                P.inc("tensor", lambda E, b=b, dt=dt: E.matmul(
                    pb[1].ap()[0:1, :], scb[:, 0:1], rx2[b][:, :],
                    start=(dt == 0), stop=(dt == NDT - 1)), "r_smm", 1)
            P.wait("vector", "r_smm", P.val("r_smm"))
            if i > 0:
                P.raw("vector", lambda E: E.tensor_add(totR[:, :], totR[:, :],
                                                       totc[:, :]))
                P.raw("vector", lambda E: E.tensor_add(colR[:, :], colR[:, :],
                                                       colc[:, :]))
            P.raw("vector", lambda E: E.tensor_scalar(mR[:, :], pb[0].ap()[0:1, :],
                                                      1.0 / D, None, ALU.mult))
            P.raw("vector", lambda E: E.tensor_scalar(e2R[:, :], pb[1].ap()[0:1, :],
                                                      1.0 / D, None, ALU.mult))
            P.raw("vector", lambda E: E.tensor_mul(vR[:, :], mR[:, :], mR[:, :]))
            P.raw("vector", lambda E: E.tensor_sub(vR[:, :], e2R[:, :], vR[:, :]))
            vv = P.inc("vector", lambda E: E.tensor_scalar(
                vR[:, :], vR[:, :], LN_EPS, None, ALU.add), "r_var", 1)
            P.wait("scalar", "r_var", vv)
            sq = P.inc("scalar", lambda E: E.activation(
                out=sR[:, :], in_=vR[:, :], func=AF.Sqrt, scale=c0(1, 1),
                bias=c0(0, 1)), "r_sqt", 1)
            P.wait("vector", "r_sqt", sq)
            rv = P.inc("vector", lambda E: E.reciprocal(rsR[:, :], sR[:, :]),
                       "r_rs", 1)
            P.wait("tensor", "r_rs", rv)
            P.wait("tensor", "r_xn", P.val("r_xn"))
            P.raw("tensor", lambda E: E.matmul(pb[2].ap()[:, :], sonesrow[:, :],
                                               mR[:, :], start=True, stop=True))
            bc = P.inc("tensor", lambda E: E.matmul(
                pb[3].ap()[:, :], sonesrow[:, :], rsR[:, :], start=True,
                stop=True), "r_bc", 1)
            P.wait("vector", "r_bc", bc)
            for dt in range(NDT):
                P.raw("vector", lambda E, dt=dt: E.tensor_sub(
                    tbufR[:, :], sxR[:, dt, :], pb[2].ap()[:, :]))
                P.raw("vector", lambda E, dt=dt: E.tensor_mul(
                    xnR[:, dt, :], tbufR[:, :], pb[3].ap()[:, :]))
            xnv = P.inc("vector", lambda E: E.tensor_copy(bcell_v, bcell_v),
                        "r_xn", 1)
            for j in range(NJ):
                bank_k = 4 + 2 * (j % 2)
                bank_v = 5 + 2 * (j % 2)
                P.wait("tensor", "r_xn", xnv)
                P.wait("tensor", "r_ev", P.val("r_ev") - 2)
                for dt in range(NDT):
                    bld = lambda E, j=j, dt=dt, bank=bank_k: E.matmul(
                        pb[bank].ap()[:, :], wkv[:, j, dt, :], xnR[:, dt, :],
                        start=(dt == 0), stop=(dt == NDT - 1))
                    P.raw("tensor", bld)
                for dt in range(NDT):
                    bld = lambda E, j=j, dt=dt, bank=bank_v: E.matmul(
                        pb[bank].ap()[:, :], wkv[:, 8 + j, dt, :], xnR[:, dt, :],
                        start=(dt == 0), stop=(dt == NDT - 1))
                    if dt == NDT - 1:
                        kvv = P.inc("tensor", bld, "r_kv", 1)
                    else:
                        P.raw("tensor", bld)
                P.wait("scalar", "r_kv", kvv)
                P.wait("scalar", "r_sq", P.val("r_sq"))
                P.raw("scalar", lambda E, bank=bank_k: E.activation(
                    out=ekR[:, :], in_=pb[bank].ap()[:, :], func=AF.Copy,
                    scale=0.125, bias=0.0))
                ev = P.inc("scalar", lambda E, bank=bank_v: E.copy(
                    out=evR[:, :], in_=pb[bank].ap()[:, :]), "r_ev", 2)
                P.wait("vector", "r_ev", ev)
                ck = P.inc("vector", lambda E: E.tensor_scalar(
                    bxkR[:, :], ekR[:, :], CLIP_HI, -CLIP_HI, ALU.min, ALU.max),
                    "r_ck", 1)
                P.wait("scalar", "r_ck", ck)
                sqv = P.inc("scalar", lambda E: E.activation(
                    out=ykR[:, :], in_=bxkR[:, :], func=AF.Square, scale=c0(1),
                    bias=c0(0)), "r_sq", 1)
                P.wait("vector", "r_sq", sqv)
                for p in range(1, 6):
                    cc = colc[:, j * NP + p - 1:j * NP + p]
                    tc = totc[:, j * NP + p - 1:j * NP + p]
                    if p == 1:
                        P.raw("vector", lambda E, A=cc: E.scalar_tensor_tensor(
                            tkR[:, :], bxkR[:, :], 1.0, onepl[:, :], ALU.mult,
                            ALU.mult, accum_out=A))
                    elif p == 2:
                        P.raw("vector", lambda E, A=cc: E.scalar_tensor_tensor(
                            tkR[:, :], ykR[:, :], 2.0, negpl[:, :], ALU.mult,
                            ALU.add, accum_out=A))
                    elif p == 3:
                        P.raw("vector", lambda E: E.tensor_scalar(
                            tmR[:, :], ykR[:, :], 4.0, -3.0, ALU.mult, ALU.add))
                        P.raw("vector", lambda E, A=cc: E.scalar_tensor_tensor(
                            tkR[:, :], tmR[:, :], 1.0, bxkR[:, :], ALU.mult,
                            ALU.mult, accum_out=A))
                    elif p == 4:
                        P.raw("vector", lambda E: E.tensor_scalar(
                            tmR[:, :], ykR[:, :], -1.0, 8.0, ALU.add, ALU.mult))
                        P.raw("vector", lambda E: E.tensor_mul(
                            tmR[:, :], tmR[:, :], ykR[:, :]))
                        P.raw("vector", lambda E, A=cc: E.scalar_tensor_tensor(
                            tkR[:, :], tmR[:, :], 1.0, onepl[:, :], ALU.mult,
                            ALU.add, accum_out=A))
                    elif p == 5:
                        P.raw("vector", lambda E: E.tensor_scalar(
                            tmR[:, :], ykR[:, :], 16.0, -20.0, ALU.mult, ALU.add))
                        P.raw("vector", lambda E: E.tensor_mul(
                            tmR[:, :], tmR[:, :], ykR[:, :]))
                        P.raw("vector", lambda E: E.tensor_scalar(
                            tmR[:, :], tmR[:, :], 5.0, None, ALU.add))
                        P.raw("vector", lambda E, A=cc: E.scalar_tensor_tensor(
                            tkR[:, :], tmR[:, :], 1.0, bxkR[:, :], ALU.mult,
                            ALU.mult, accum_out=A))
                    P.raw("vector", lambda E, A=tc: E.scalar_tensor_tensor(
                        tvS[:, :], tkR[:, :], 1.0, evR[:, :], ALU.mult, ALU.mult,
                        accum_out=A))
            tva = P.inc("vector", lambda E: E.tensor_copy(bcell_v, bcell_v),
                        "r_tva", 1)
            P.wait("scalar", "r_tva", tva)
            tvb = P.inc("scalar", lambda E: E.copy(out=bcell_s, in_=bcell_s),
                        "r_tvb", 1)
            P.wait("vector", "r_tvb", tvb)
            P.guard_end()
        P.raw("vector", lambda E: E.tensor_add(totR[:, :], totR[:, :],
                                               totc[:, :]))
        P.raw("vector", lambda E: E.tensor_add(colR[:, :], colR[:, :],
                                               colc[:, :]))
        cb = P.inc("vector", lambda E: E.tensor_copy(colRb[:, :], colR[:, :]),
                   "r_cb", 1)
        P.wait("tensor", "r_cb", cb)
        km = P.inc("tensor", lambda E: E.matmul(
            pb[0].ap()[0:2, 0:40], scb[:, 1:3], colRb[:, :], start=True,
            stop=True), "r_km", 1)
        P.wait("scalar", "r_km", km)
        ke = P.inc("scalar", lambda E: E.copy(out=kstR[:, :],
                                              in_=pb[0].ap()[0:2, 0:40]),
                   "r_ke", 1)
        cxR.close()

        rr = P.inc("vector", lambda E: E.tensor_copy(bcell_v, bcell_v), "r_rdy", 1)
        P.wait("sync", "r_rdy", rr)
        P.wait("sync", "r_ke", ke)
        P.guard_begin(("sync",), {"sync": regs_npar["sync"]})
        P.inc("sync", lambda E: E.dma_start(out=shR_kv[1].ap()[:, :],
                                            in_=totR[:, :]), "d_post", 16)
        pv = P.inc("sync", lambda E: E.dma_start(out=shR_ks[1].ap()[:, :],
                                                 in_=kstR[:, :]), "d_post", 16)
        P.wait("sync", "d_post", pv)
        P.inc("sync", lambda E: E.dma_start(out=shRf.ap()[0:1, 1:2],
                                            in_=stok32[:, :]), "d_post", 16)
        P.guard_end()
        P.guard_begin(("sync",), {"sync": regs_par["sync"]})
        poll(shRf.ap()[0:1, 1:2])
        P.inc("sync", lambda E: E.dma_start(out=tmpkv[:, :],
                                            in_=shR_kv[1].ap()[:, :]), "d_geto", 16)
        P.inc("sync", lambda E: E.dma_start(out=tmpks[:, :],
                                            in_=shR_ks[1].ap()[:, :]), "d_geto", 16)
        gv = P.val("d_geto")
        P.wait("sync", "d_geto", gv)
        P.inc("sync", lambda E: E.nop(), "c_part", 1)
        P.guard_end()
        cpv = P.val("c_part")
        P.guard_begin(("vector",), {"vector": regs_par["vector"]})
        P.wait("vector", "c_part", cpv)
        P.raw("vector", lambda E: E.tensor_add(carry_src[:, :],
                                               totR[:, :], tmpkv[:, :]))
        P.inc("vector", lambda E: E.tensor_add(kstR2[:, :], kstR[:, :],
                                               tmpks[:, :]), "c_carry", 1)
        P.guard_end()
        ccv = P.val("c_carry")
        P.guard_begin(("sync",), {"sync": regs_par["sync"]})
        P.wait("sync", "c_carry", ccv)
        for j in range(NJ):
            for r in range(2):
                P.inc("sync", lambda E, j=j, r=r: E.dma_start(
                    out=ksc16[2 * j + r:2 * j + r + 1, :],
                    in_=kstR2[r:r + 1, j * NP:(j + 1) * NP]), "d_ksc", 16)
        P.wait("sync", "d_ksc", P.val("d_ksc"))
        P.inc("sync", lambda E: E.nop(), "c_evcar", 1)
        P.guard_end()
        return P.val("c_evcar")

    if fused:
        EV_CARRY = emit_r_phase_and_exchange()
        carry_kv = carry_src
        P.barrier(bcell_v, bcell_s)
    if main and not fused:
        P.wait("vector", "d_misc", misc_n * 16)
        P.raw("vector", lambda E: E.memset(cacc[0][:, :], 0.0))
        P.raw("vector", lambda E: E.memset(kacc[0][:, :], 0.0))
        a = 0
        for r in range(NCORES):
            P.raw("vector", lambda E, r=r, a=a: E.scalar_tensor_tensor(
                cacc[1 - a][:, :], stot[:, r, :], smask[:, r:r + 1], cacc[a][:, :],
                ALU.mult, ALU.add))
            P.inc("vector", lambda E, r=r, a=a: E.scalar_tensor_tensor(
                kacc[1 - a][:, :], sks[:, r, :], sksmask[:, r:r + 1], kacc[a][:, :],
                ALU.mult, ALU.add), "s_carry", 1)
            a = 1 - a
        carry_kv = cacc[a]
        carry_ks2 = kacc[a]
        CARRY_V = P.val("s_carry")
        P.wait("gpsimd", "s_carry", CARRY_V)
        for j in range(NJ):
            for r in range(2):
                P.inc("gpsimd", lambda E, j=j, r=r: E.dma_start(
                    out=ksc16[2 * j + r:2 * j + r + 1, :],
                    in_=carry_ks2[r:r + 1, j * NP:(j + 1) * NP]), "d_rel", 16)
        KSC_DONE = P.val("d_rel")

    cxA = ExitStack()
    sx = [sb(cxA, f"sx{i}", [128, S]) for i in range(NDT)]
    xbf = [sb(cxA, f"xbf{i}", [128, 512], BF16) for i in range(2)]
    x2bf = [sb(cxA, f"x2bf{i}", [128, 512], BF16) for i in range(2)]
    tbuf = sb(cxA, "tbuf", [128, 512])
    mrow = sb(cxA, "mrow", [1, 512])
    e2row = sb(cxA, "e2row", [1, 512])
    vrow = sb(cxA, "vrow", [1, 512])
    srow = sb(cxA, "srow", [1, 512])
    rsrow = sb(cxA, "rsrow", [1, 512])

    xld = [ld(sx[dt][:, :], xT.ap()[dt * 128:(dt + 1) * 128, :], "d_x")
           for dt in range(NDT)]

    ln_mm = {}
    gi = 0
    for ch in range(NCH):
        for dt in range(NDT):
            b = dt % 2
            P.wait("scalar", "d_x", xld[dt])
            if gi == 0:
                P.wait("scalar", "d_misc", 3 * 16)
            if gi >= 2:
                P.wait("scalar", "t_ln", ln_mm[gi - 2])
            P.raw("scalar", lambda E, dt=dt, ch=ch, b=b: E.copy(
                out=xbf[b][:, :], in_=sx[dt][:, ch * 512:(ch + 1) * 512]))
            cv = P.inc("scalar", lambda E, dt=dt, ch=ch, b=b: E.activation(
                out=x2bf[b][:, :], in_=sx[dt][:, ch * 512:(ch + 1) * 512],
                func=AF.Square, scale=c0(1), bias=c0(0)), "s_cv", 1)
            P.wait("tensor", "s_cv", cv)
            P.raw("tensor", lambda E, dt=dt, b=b: E.matmul(
                pb[0].ap()[0:1, :], scb[:, 0:1], xbf[b][:, :],
                start=(dt == 0), stop=(dt == NDT - 1)))
            ln_mm[gi] = P.inc("tensor", lambda E, dt=dt, b=b: E.matmul(
                pb[1].ap()[0:1, :], scb[:, 0:1], x2bf[b][:, :],
                start=(dt == 0), stop=(dt == NDT - 1)), "t_ln", 1)
            gi += 1
        P.wait("vector", "t_ln", P.val("t_ln"))
        P.raw("vector", lambda E: E.tensor_scalar(mrow[:, :], pb[0].ap()[0:1, :],
                                                  1.0 / D, None, ALU.mult))
        P.raw("vector", lambda E: E.tensor_scalar(e2row[:, :], pb[1].ap()[0:1, :],
                                                  1.0 / D, None, ALU.mult))
        P.raw("vector", lambda E: E.tensor_mul(vrow[:, :], mrow[:, :], mrow[:, :]))
        P.raw("vector", lambda E: E.tensor_sub(vrow[:, :], e2row[:, :], vrow[:, :]))
        vv = P.inc("vector", lambda E: E.tensor_scalar(
            vrow[:, :], vrow[:, :], LN_EPS, None, ALU.add), "s_var", 1)
        P.wait("scalar", "s_var", vv)
        sq = P.inc("scalar", lambda E: E.activation(
            out=srow[:, :], in_=vrow[:, :], func=AF.Sqrt, scale=c0(1, 1),
            bias=c0(0, 1)), "s_sqrt", 1)
        P.wait("vector", "s_sqrt", sq)
        rv = P.inc("vector", lambda E: E.reciprocal(rsrow[:, :], srow[:, :]),
                   "s_rs", 1)
        P.wait("tensor", "s_rs", rv)
        P.raw("tensor", lambda E: E.matmul(pb[2].ap()[:, :], sonesrow[:, :],
                                           mrow[:, :], start=True, stop=True))
        bc = P.inc("tensor", lambda E: E.matmul(pb[3].ap()[:, :], sonesrow[:, :],
                                                rsrow[:, :], start=True, stop=True),
                   "t_bc", 1)
        P.wait("vector", "t_bc", bc)
        for dt in range(NDT):
            P.raw("vector", lambda E, dt=dt, ch=ch: E.tensor_sub(
                tbuf[:, :], sx[dt][:, ch * 512:(ch + 1) * 512], pb[2].ap()[:, :]))
            P.raw("vector", lambda E, dt=dt, ch=ch: E.tensor_mul(
                xn[dt][:, ch * 512:(ch + 1) * 512], tbuf[:, :], pb[3].ap()[:, :]))
        xch = P.inc("vector", lambda E: E.tensor_copy(bcell_v, bcell_v), "s_xnch", 1)
        P.wait("tensor", "s_xnch", xch)
    P.barrier(bcell_v, bcell_s)
    cxA.close()

    def _finish():
        with nc.Block() as block:
            P.realize(block)
        return nc

    if stop_after == "ln":
        return _finish()

    cxH = ExitStack()
    NH_IN = 15
    if main:
        hdn = [sb(cxH, f"hdn{t}", [128, S], BF16) for t in range(NH_IN)]
        wg_b = [sb(cxH, f"wg_b{i}", [128, NDT, 128], BF16) for i in range(2)]
        wu_b = [sb(cxH, f"wu_b{i}", [128, NDT, 128], BF16) for i in range(2)]
        sgb = [sb(cxH, f"sgb{i}", [128, 512]) for i in range(2)]
        ubf = [sb(cxH, f"ubf{i}", [128, 512]) for i in range(2)]
    cxB = ExitStack()
    xkf = sb(cxB, "xkf", [128, S], BF16)
    xqf = sb(cxB, "xqf", [128, S], BF16)
    ykf = sb(cxB, "ykf", [128, S], BF16)
    yqf = sb(cxB, "yqf", [128, S], BF16)
    bv = sb(cxB, "bv", [128, S], BF16)
    tkr = sb(cxB, "tkr", [128, 2, S], BF16)
    tqr = sb(cxB, "tqr", [128, 2, S], BF16)
    ctmp = sb(cxB, "ctmp", [128, 512], BF16) if not main else None
    ctv = sb(cxB, "ctv", [128, S], BF16)
    ckvp = sb(cxB, "ckvp", [128, S])
    sstate = sb(cxB, "sstate", [128, NP])
    if main:
        kqst = [sb(cxB, f"kqst{i}", [2, NCH * 512], BF16) for i in range(2)]

    def proj(buf_w, wv, dst, scale, banks):
        P.wait("tensor", "d_w", wv)
        P.wait("tensor", "s_ev", P.val("s_ev"))
        P.wait("tensor", "s_kse", P.val("s_kse"))
        P.wait("scalar", "s_tkp", P.val("s_tkp"))
        for ch in range(NCH):
            for dt in range(NDT):
                bank = banks[ch]
                bld = lambda E, b=buf_w, dt=dt, ch=ch, bank=bank: E.matmul(
                    pb[bank].ap()[:, :], b[:, dt, :],
                    xn[dt][:, ch * 512:(ch + 1) * 512],
                    start=(dt == 0), stop=(dt == NDT - 1))
                if dt == NDT - 1 and ch == NCH - 1:
                    mv = P.inc("tensor", bld, "t_mm", 1)
                else:
                    P.raw("tensor", bld)
        wlast[id(buf_w)] = ("t_mm", mv)
        P.wait("scalar", "t_mm", mv)
        ev = 0
        for ch in range(NCH):
            if scale is None:
                ev = P.inc("scalar", lambda E, ch=ch, bank=banks[ch], d=dst: E.copy(
                    out=d[:, ch * 512:(ch + 1) * 512], in_=pb[bank].ap()[:, :]),
                    "s_ev", 1)
            else:
                ev = P.inc("scalar", lambda E, ch=ch, bank=banks[ch], d=dst,
                           s=scale: E.activation(
                    out=d[:, ch * 512:(ch + 1) * 512], in_=pb[bank].ap()[:, :],
                    func=AF.Copy, scale=s, bias=0.0), "s_ev", 1)
        return ev

    def cheb(y, x, p, dst, tmp, ch, fin=None):
        sl = ((lambda t: t[:, :]) if ch is None
              else (lambda t: t[:, ch * 512:(ch + 1) * 512]))
        em = ((lambda b: P.inc("vector", b, fin[0], fin[1])) if fin
              else (lambda b: P.raw("vector", b)))
        if p == 2:
            em(lambda E: E.tensor_scalar(
                dst, sl(y), 2.0, -1.0, ALU.mult, ALU.add))
        elif p == 3:
            P.raw("vector", lambda E: E.tensor_scalar(
                tmp[:, :], sl(y), 4.0, -3.0, ALU.mult, ALU.add))
            em(lambda E: E.tensor_mul(dst, tmp[:, :], sl(x)))
        elif p == 4:
            P.raw("vector", lambda E: E.tensor_scalar(
                tmp[:, :], sl(y), -1.0, 8.0, ALU.add, ALU.mult))
            P.raw("vector", lambda E: E.tensor_mul(tmp[:, :], tmp[:, :], sl(y)))
            em(lambda E: E.tensor_scalar(
                dst, tmp[:, :], 1.0, None, ALU.add))
        elif p == 5:
            P.raw("vector", lambda E: E.tensor_scalar(
                tmp[:, :], sl(y), 16.0, -20.0, ALU.mult, ALU.add))
            P.raw("vector", lambda E: E.tensor_mul(tmp[:, :], tmp[:, :], sl(y)))
            P.raw("vector", lambda E: E.tensor_scalar(
                tmp[:, :], tmp[:, :], 5.0, None, ALU.add))
            em(lambda E: E.tensor_mul(dst, tmp[:, :], sl(x)))

    INV2 = INV127 * INV127

    def ffn_ht_post(ht):
        bi = ht % 2
        wgv = wload(w_g, ht, wg_b[bi])
        wuv = wload(w_u, ht, wu_b[bi])
        P.wait("tensor", "d_w", wgv)
        P.wait("tensor", "s_sg", P.val("s_sg"))
        P.wait("tensor", "s_rde", P.val("s_rde"))
        for ch in range(NCH):
            for dt in range(NDT):
                bld = lambda E, dt=dt, ch=ch, bi=bi: E.matmul(
                    pb[4 + ch].ap()[:, :], wg_b[bi][:, dt, :],
                    xn[dt][:, ch * 512:(ch + 1) * 512],
                    start=(dt == 0), stop=(dt == NDT - 1))
                if dt == NDT - 1 and ch == NCH - 1:
                    gv = P.inc("tensor", bld, "t_g", 1)
                else:
                    P.raw("tensor", bld)
        wlast[id(wg_b[bi])] = ("t_g", gv)
        P.wait("scalar", "t_g", gv)
        sgv = {}
        stv = {}
        h_base = P.val("s_h")
        for ch in range(2):
            sgv[ch] = P.inc("scalar", lambda E, ch=ch: E.activation(
                out=sgb[ch % 2][:, :], in_=pb[4 + ch].ap()[:, :],
                func=AF.Silu, scale=c0(2), bias=c0(0)), "s_sg", 1)
        P.wait("tensor", "d_w", wuv)
        P.wait("tensor", "s_h", h_base)
        uv = {}
        for ch in range(NCH):
            for dt in range(NDT):
                bld = lambda E, dt=dt, ch=ch, bi=bi: E.matmul(
                    pb[ch].ap()[:, :], wu_b[bi][:, dt, :],
                    xn[dt][:, ch * 512:(ch + 1) * 512],
                    start=(dt == 0), stop=(dt == NDT - 1))
                if dt == NDT - 1:
                    uv[ch] = P.inc("tensor", bld, "t_u", 1)
                else:
                    P.raw("tensor", bld)
        wlast[id(wu_b[bi])] = ("t_u", uv[NCH - 1])
        for ch in range(NCH):
            if ch >= 2:
                P.wait("scalar", "s_h", stv[ch - 2])
                sgv[ch] = P.inc("scalar", lambda E, ch=ch: E.activation(
                    out=sgb[ch % 2][:, :], in_=pb[4 + ch].ap()[:, :],
                    func=AF.Silu, scale=c0(2), bias=c0(0)), "s_sg", 1)
            P.wait("vector", "s_sg", sgv[ch])
            P.wait("vector", "t_u", uv[ch])
            stv[ch] = P.inc("vector", lambda E, ht=ht, ch=ch:
                            E.scalar_tensor_tensor(
                hdn[ht][:, ch * 512:(ch + 1) * 512], sgb[ch % 2][:, :], INV2,
                pb[ch].ap()[:, :], ALU.mult, ALU.mult), "s_h", 1)

    def ffn_ht(ht, pre=None):
        bi = ht % 2
        if pre is None:
            wgv = wload(w_g, ht, wg_b[bi])
            wuv = wload(w_u, ht, wu_b[bi])
        else:
            wgv, wuv = pre
        P.wait("tensor", "d_w", wgv)
        P.wait("tensor", "s_h", P.val("s_h"))
        for ch in range(NCH):
            for dt in range(NDT):
                bld = lambda E, dt=dt, ch=ch, bi=bi: E.matmul(
                    pb[4 + ch].ap()[:, :], wg_b[bi][:, dt, :],
                    xn[dt][:, ch * 512:(ch + 1) * 512],
                    start=(dt == 0), stop=(dt == NDT - 1))
                if dt == NDT - 1 and ch == NCH - 1:
                    gv = P.inc("tensor", bld, "t_g", 1)
                else:
                    P.raw("tensor", bld)
        wlast[id(wg_b[bi])] = ("t_g", gv)
        P.wait("scalar", "t_g", gv)
        P.wait("scalar", "s_h", P.val("s_h"))
        P.wait("tensor", "d_w", wuv)
        for half in range(2):
            chs = (2 * half, 2 * half + 1)
            sgv = {}
            if half == 1:
                P.wait("scalar", "s_h", P.val("s_h"))
            for ch in chs:
                sgv[ch] = P.inc("scalar", lambda E, ch=ch: E.activation(
                    out=sgb[ch % 2][:, :], in_=pb[4 + ch].ap()[:, :],
                    func=AF.Silu, scale=c0(2), bias=c0(0)), "s_sg", 1)
            P.wait("tensor", "s_sg", sgv[chs[1]])
            for ch in chs:
                for dt in range(NDT):
                    bld = lambda E, dt=dt, ch=ch, bi=bi: E.matmul(
                        pb[4 + ch].ap()[:, :], wu_b[bi][:, dt, :],
                        xn[dt][:, ch * 512:(ch + 1) * 512],
                        start=(dt == 0), stop=(dt == NDT - 1))
                    if dt == NDT - 1 and ch == chs[1]:
                        uv = P.inc("tensor", bld, "t_u", 1)
                    else:
                        P.raw("tensor", bld)
            wlast[id(wu_b[bi])] = ("t_u", uv)
            P.wait("scalar", "t_u", uv)
            for ch in chs:
                ue = P.inc("scalar", lambda E, ch=ch: E.activation(
                    out=ubf[ch % 2][:, :], in_=pb[4 + ch].ap()[:, :],
                    func=AF.Copy, scale=INV2, bias=0.0), "s_ue", 1)
                P.wait("gpsimd", "s_ue", ue)
                P.wait("gpsimd", "s_sg", sgv[ch])
                P.inc("gpsimd", lambda E, ht=ht, ch=ch: E.tensor_mul(
                    hdn[ht][:, ch * 512:(ch + 1) * 512], sgb[ch % 2][:, :],
                    ubf[ch % 2][:, :]), "s_h", 1)

    HT_SCHED = []
    _c = [2, 2, 2, 2, 2, 2, 2, 1]
    _ht = 0
    for j in range(NJ):
        HT_SCHED.append(list(range(_ht, _ht + _c[j])))
        _ht += _c[j]

    ks_base = [0]

    _scnt = [0]

    def sum_side(j, pairs, tile_base):
        for side, p in pairs:
            tb_ = tkr if side == "k" else tqr
            P.wait("tensor", "s_tkp", tile_base[0] + p)
            pp = _scnt[0] % 2
            _scnt[0] += 1
            if main:
                P.wait("scalar", "d_scr", P.val("d_scr") - 16)
            for ch in range(NCH):
                bank = ch % 2
                if p == 1 and main:
                    src = (xkf if side == "k" else xqf)[
                        :, ch * 512:(ch + 1) * 512]
                else:
                    src = tb_[:, (p - 1) % 2, ch * 512:(ch + 1) * 512]
                P.wait("tensor", "s_kse", P.val("s_kse") - 1)
                kv_ = P.inc("tensor", lambda E, src=src, bank=bank: E.matmul(
                    pb[bank].ap()[0:2, :], scb[:, 1:3], src,
                    start=True, stop=True), "t_ks", 1)
                P.wait("scalar", "t_ks", kv_)
                if main:
                    P.inc("scalar", lambda E, pp=pp, ch=ch, bank=bank:
                          E.copy(out=kqst[pp][:, ch * 512:(ch + 1) * 512],
                                 in_=pb[bank].ap()[0:2, :]), "s_kse", 1)
                else:
                    P.inc("scalar", lambda E, c=bcell_s: E.copy(
                        out=c, in_=c), "s_kse", 1)
            if main:
                scr = kscr if side == "k" else qscr
                P.inc("scalar", lambda E, scr=scr, j=j, p=p, pp=pp:
                      E.dma_start(out=scr.ap()[2 * j:2 * j + 2, p - 1, :],
                                  in_=kqst[pp][:, :]), "d_scr", 16)

    for j in range(NJ):
        pre_w = {}
        kt = (8 + j) if main else j
        vt = (16 + j) if main else (8 + j)
        wsrc = w_qkv if main else w_kv
        wvk = wload(wsrc, kt, wq[0])
        ekv = proj(wq[0], wvk, xkf, 0.125, (0, 1, 2, 3))
        P.wait("vector", "s_ev", ekv)
        P.wait("vector", "s_sq", P.val("s_sq"))
        ck = P.inc("vector", lambda E: E.tensor_scalar(
            xkf[:, :], xkf[:, :], CLIP_HI, -CLIP_HI, ALU.min, ALU.max),
            "s_ck", 1)
        P.wait("scalar", "s_ck", ck)
        P.inc("scalar", lambda E: E.activation(
            out=ykf[:, :], in_=xkf[:, :], func=AF.Square, scale=c0(1),
            bias=c0(0)), "s_sq", 1)
        wvv = wload(wsrc, vt, wq[1])
        evv = proj(wq[1], wvv, bv, None, (0, 1, 2, 3))
        if main:
            wvq = wload(wsrc, j, wq[0])
            eqv = proj(wq[0], wvq, xqf, 0.125, (0, 1, 2, 3))
            P.wait("vector", "s_ev", eqv)
            cq = P.inc("vector", lambda E: E.tensor_scalar(
                xqf[:, :], xqf[:, :], CLIP_HI, -CLIP_HI, ALU.min, ALU.max),
                "s_ck", 1)
            P.wait("scalar", "s_ck", cq)
            P.inc("scalar", lambda E: E.activation(
                out=yqf[:, :], in_=xqf[:, :], func=AF.Square, scale=c0(1),
                bias=c0(0)), "s_sq", 1)
        P.wait("vector", "s_sq", P.val("s_sq"))
        P.wait("vector", "s_ev", evv)
        if fused:
            P.guard_begin(("sync",), {"sync": regs_npar["sync"]})
            poll_fn(shSSf.ap()[0:1, j:j + 1])
            gv_ = P.inc("sync", lambda E, j=j: E.dma_start(
                out=carry_src[:, j * NP:(j + 1) * NP],
                in_=shSS.ap()[:, j * NP:(j + 1) * NP]), "d_codd", 16)
            P.wait("sync", "d_codd", gv_)
            P.inc("sync", lambda E: E.nop(), "c_rdy", 1)
            P.guard_end()
            P.wait("vector", "c_rdy", P.val("c_rdy"))
            P.wait("vector", "d_ss", P.val("d_ss"))
        if main and not fused:
            P.wait("vector", "s_carry", CARRY_V)
            P.wait("vector", "d_rel", KSC_DONE)
        ks_tile_base = P.val("s_tkp")
        tbh = [ks_tile_base]
        if main:
            for p in range(1, 6):
                if p >= 2:
                    P.wait("vector", "t_ks", ks_base[0] + 8 * (p - 2))
                if p == 1:
                    ksrc = xkf[:, :]
                else:
                    ktk = tkr[:, (p - 1) % 2, :]
                    cheb(ykf, xkf, p, ktk, ctv, None)
                    ksrc = ktk
                P.raw("vector", lambda E, s=ksrc: E.tensor_mul(
                    ctv[:, :], s, bv[:, :]))
                init = carry_kv[:, j * NP + (p - 1):j * NP + p]
                P.raw("vector", lambda E, init=init: E.tensor_tensor_scan(
                    ckvp[:, :], ctv[:, :], ctv[:, :], init, ALU.add,
                    ALU.bypass))
                P.raw("vector", lambda E, p=p: E.tensor_copy(
                    sstate[:, p - 1:p], ckvp[:, S - 1:S]))
                if p == 1:
                    P.raw("vector", lambda E, j=j: E.scalar_tensor_tensor(
                        num[j][:, :], xqf[:, :], BETA[0], ckvp[:, :], ALU.mult,
                        ALU.mult))
                else:
                    qtk = tqr[:, (p - 1) % 2, :]
                    cheb(yqf, xqf, p, qtk, ctv, None)
                    P.raw("vector", lambda E, s=qtk, p=p:
                          E.scalar_tensor_tensor(
                        ctv[:, :], s, BETA[p - 1], ckvp[:, :], ALU.mult,
                        ALU.mult))
                    P.raw("vector", lambda E, j=j: E.tensor_add(
                        num[j][:, :], num[j][:, :], ctv[:, :]))
                P.inc("vector", lambda E: E.tensor_copy(bcell_v, bcell_v),
                      "s_tkp", 1)
        else:
            for p in range(1, 6):
                if p >= 3:
                    P.wait("vector", "t_ks", ks_base[0] + 4 * (p - 2))
                for ch in range(NCH):
                    c5 = slice(ch * 512, (ch + 1) * 512)
                    ktk = tkr[:, (p - 1) % 2, c5]
                    if p == 1:
                        P.raw("vector", lambda E, s=ktk, c5=c5: E.tensor_copy(
                            s, xkf[:, c5]))
                    else:
                        cheb(ykf, xkf, p, ktk, ctmp, ch)
                    P.raw("vector", lambda E, s=ktk, c5=c5, p=p:
                          E.scalar_tensor_tensor(
                        tqr[:, (p - 1) % 2, c5], s, 1.0, bv[:, c5], ALU.mult,
                        ALU.mult))
                jp = j * NP + (p - 1)
                P.raw("vector", lambda E, jp=jp, p=p: E.tensor_reduce(
                    tot_kv[:, jp:jp + 1], tqr[:, (p - 1) % 2, :],
                    mybir.AxisListType.X, ALU.add))
                P.raw("vector", lambda E, p=p: E.tensor_reduce(
                    colT[:, p:p + 1], tkr[:, (p - 1) % 2, :],
                    mybir.AxisListType.X, ALU.add))
                P.inc("vector", lambda E: E.tensor_copy(bcell_v, bcell_v),
                      "s_tkp", 1)
        if main:
            hts = HT_SCHED[j]
            if len(hts) >= 1:
                ffn_ht(hts[0], pre_w.get(hts[0]))
            sum_side(j, [("k", 1), ("q", 1), ("k", 2), ("q", 2)], tbh)
            if len(hts) >= 2:
                ffn_ht(hts[1], pre_w.get(hts[1]))
            sum_side(j, [("k", 3), ("q", 3), ("k", 4), ("q", 4)], tbh)
            if len(hts) >= 3:
                ffn_ht(hts[2])
            sum_side(j, [("k", 5), ("q", 5)], tbh)
        else:
            sum_side(j, [("k", p) for p in range(1, 6)], tbh)
        ks_base[0] = P.val("t_ks")
        if main:
            tkc_now = P.val("s_tkp")
            if fused:
                P.guard_begin(("sync",), {"sync": regs_par["sync"]})
                P.wait("sync", "s_tkp", tkc_now)
                sv_ = P.inc("sync", lambda E, j=j: E.dma_start(
                    out=shSS.ap()[:, j * NP:(j + 1) * NP], in_=sstate[:, :]),
                    "d_ss", 16)
                P.wait("sync", "d_ss", sv_)
                P.inc("sync", lambda E, j=j: E.dma_start(
                    out=shSSf.ap()[0:1, j:j + 1], in_=stok32[:, :]), "d_ss", 16)
                P.guard_end()
        if not main:
            tv_ = P.inc("vector", lambda E: E.tensor_copy(bcell_v, bcell_v),
                        "s_colT", 1)
            P.wait("scalar", "s_colT", tv_)
            cb = P.inc("scalar", lambda E: E.copy(out=colTb[:, 1:6],
                                                  in_=colT[:, 1:6]), "s_ctb", 1)
            P.wait("tensor", "s_ctb", cb)
            P.wait("tensor", "s_kse", P.val("s_kse"))
            ksm = P.inc("tensor", lambda E, j=j: E.matmul(
                pb[2].ap()[0:2, 0:5], scb[:, 1:3], colTb[:, 1:6],
                start=True, stop=True), "t_kst", 1)
            P.wait("scalar", "t_kst", ksm)
            P.inc("scalar", lambda E, j=j: E.copy(
                out=kst2[:, j * NP:(j + 1) * NP], in_=pb[2].ap()[0:2, 0:5]),
                "s_kst", 1)
        if stop_after is not None:
            P.barrier(bcell_v, bcell_s)
        elif j % 3 == 2 or j == NJ - 1:
            b4 = P.val("bar4") + 4
            P.inc("tensor", lambda E: E.nop(), "bar4", 1)
            P.inc("gpsimd", lambda E: E.engine_nop(), "bar4", 1)
            P.inc("vector", lambda E: E.tensor_copy(bcell_v, bcell_v),
                  "bar4", 1)
            P.inc("scalar", lambda E: E.copy(out=bcell_s, in_=bcell_s),
                  "bar4", 1)
            for e in ("tensor", "scalar", "vector", "gpsimd"):
                P.wait(e, "bar4", b4)
        if stop_after == f"tile{j}":
            return _finish()

    if not main:
        tf = P.inc("vector", lambda E: E.tensor_copy(bcell_v, bcell_v), "s_fin", 1)
        P.wait("sync", "s_fin", tf)
        P.wait("sync", "s_kst", P.val("s_kst"))
        P.inc("sync", lambda E: E.dma_start(out=o_tot[:, :], in_=tot_kv[:, :]),
              "d_out", 16)
        P.inc("sync", lambda E: E.dma_start(out=o_kst[:, :], in_=kst2[:, :]),
              "d_out", 16)
        P.wait("sync", "d_out", P.val("d_out"))
        cxB.close()
        cxH.close()
        with nc.Block() as block:
            P.realize(block)
        return nc

    cxB.close()
    cxD = ExitStack()
    ks16 = sb(cxD, "ks16", [16, NP, S], BF16)
    qs16 = sb(cxD, "qs16", [16, NP, S], BF16)
    kspbuf = sb(cxD, "kspbuf", [16, S], BF16)
    denb = sb(cxD, "denb", [16, S], BF16)
    rdb = sb(cxD, "rdb", [16, S], BF16)
    rdbc = sb(cxD, "rdbc", [128, 512], BF16)
    P.wait("sync", "d_scr", P.val("d_scr"))
    k16v = ld(ks16[:, :, :], kscr.ap()[:, :, :], "d_k16")
    q16v = ld(qs16[:, :, :], qscr.ap()[:, :, :], "d_k16")
    P.wait("vector", "d_k16", q16v)
    P.wait("scalar", "d_k16", q16v)
    if fused:
        P.guard_begin(("sync",), {"sync": regs_npar["sync"]})
        poll_fn(shKSf.ap()[0:1, 0:1])
        kv_ = P.inc("sync", lambda E: E.dma_start(
            out=ksc16[:, :], in_=shKS.ap()[:, :]), "d_kso", 16)
        P.wait("sync", "d_kso", kv_)
        P.inc("sync", lambda E: E.nop(), "c_ks", 1)
        P.guard_end()
        P.wait("vector", "c_ks", P.val("c_ks"))
    for p in range(1, 6):
        P.raw("vector", lambda E, p=p: E.tensor_tensor_scan(
            kspbuf[:, :],
            ks16[:, p - 1:p, :].rearrange("h one s -> h (one s)"),
            ks16[:, p - 1:p, :].rearrange("h one s -> h (one s)"),
            ksc16[:, p - 1:p], ALU.add, ALU.bypass))
        if fused:
            sv_f = P.inc("vector", lambda E: E.tensor_copy(bcell_v, bcell_v),
                         "c_sf", 1)
            P.wait("scalar", "c_sf", sv_f)
            sc_f = P.inc("scalar", lambda E, p=p: E.copy(
                out=ksfin[:, p - 1:p], in_=kspbuf[:, S - 1:S]), "s_sf", 1)
            P.wait("vector", "s_sf", sc_f)
        P.raw("vector", lambda E, p=p: E.scalar_tensor_tensor(
            kspbuf[:, :], kspbuf[:, :], BETA[p - 1],
            qs16[:, p - 1:p, :].rearrange("h one s -> h (one s)"), ALU.mult,
            ALU.mult))
        if p == 1:
            P.inc("vector", lambda E: E.tensor_copy(denb[:, :], kspbuf[:, :]),
                  "s_dka", 1)
        else:
            P.inc("vector", lambda E: E.tensor_add(denb[:, :], denb[:, :],
                                                   kspbuf[:, :]), "s_dka", 1)
    if fused:
        kfv = P.inc("vector", lambda E: E.tensor_copy(bcell_v, bcell_v),
                    "c_kf", 1)
        P.guard_begin(("sync",), {"sync": regs_par["sync"]})
        P.wait("sync", "c_kf", kfv)
        pv_ = P.inc("sync", lambda E: E.dma_start(
            out=shKS.ap()[:, :], in_=ksfin[:, :]), "d_ksp", 16)
        P.wait("sync", "d_ksp", pv_)
        P.inc("sync", lambda E: E.dma_start(
            out=shKSf.ap()[0:1, 0:1], in_=stok32[:, :]), "d_ksp", 16)
        P.guard_end()
    P.raw("vector", lambda E: E.tensor_scalar(denb[:, :], denb[:, :], EPS_DEN,
                                              None, ALU.add))
    def _recip(E):
        with nc.allow_low_precision(reason="attn denominator: bf16 suffices"):
            return E.reciprocal(rdb[:, :], denb[:, :])
    P.wait("tensor", "s_h", P.val("s_h"))
    rb = P.inc("vector", _recip, "s_rdb", 1)
    P.wait("tensor", "s_rdb", rb)
    for j in range(NJ):
        for ch in range(NCH):
            bank = 6 + (ch % 2)
            P.wait("tensor", "s_rde", P.val("s_rde") - 1)
            rm = P.inc("tensor", lambda E, j=j, ch=ch, bank=bank: E.matmul(
                pb[bank].ap()[:, :], sel_sb[:, j * 128:(j + 1) * 128],
                rdb[:, ch * 512:(ch + 1) * 512], start=True, stop=True), "t_rd", 1)
            P.wait("scalar", "t_rd", rm)
            P.wait("scalar", "s_oh", P.val("s_oh") - 1)
            re = P.inc("scalar", lambda E, bank=bank: E.copy(
                out=rdbc[:, :], in_=pb[bank].ap()[:, :]), "s_rde", 1)
            P.wait("vector", "s_rde", re)
            P.inc("vector", lambda E, j=j, ch=ch: E.tensor_mul(
                num[j][:, ch * 512:(ch + 1) * 512],
                num[j][:, ch * 512:(ch + 1) * 512], rdbc[:, :]), "s_oh", 1)
    P.barrier(bcell_v, bcell_s)
    cxD.close()
    if stop_after == "den":
        return _finish()

    cxH2 = ExitStack()
    for t in range(NH_IN, NHT):
        hdn.append(sb(cxH2, f"hdn{t}", [128, S], BF16))
    for ht in range(NH_IN, NHT):
        ffn_ht_post(ht)
    P.barrier(bcell_v, bcell_s)
    if stop_after == "ffn":
        return _finish()

    cxC2 = ExitStack()
    wdbf = [sb(cxC2, f"wdbf{i}", [128, NHT, 128], BF16) for i in range(2)]
    wobf = [sb(cxC2, f"wobf{i}", [128, NDT, 128], BF16) for i in range(2)]
    xre = [sb(cxC2, f"xre{i}", [128, 512]) for i in range(2)]

    for e2 in range(NDT):
        bi = e2 % 2
        wdv = wload(w_d, e2, wdbf[bi], nt=NHT)
        wov = wload(w_out, e2, wobf[bi])
        P.wait("tensor", "d_w", wov)
        fm = {}
        for ch in range(NCH):
            bank = ch % 2
            P.wait("sync", "s_y", P.val("s_y") - 1)
            xv = P.inc("sync", lambda E, e2=e2, ch=ch: E.dma_start(
                out=xre[ch % 2][:, :],
                in_=xT.ap()[e2 * 128:(e2 + 1) * 128, ch * 512:(ch + 1) * 512]),
                "d_xr", 16)
            P.wait("tensor", "s_y", P.val("s_y") - 1)
            for t in range(NHT):
                P.raw("tensor", lambda E, t=t, ch=ch, bank=bank, bi=bi: E.matmul(
                    pb[bank].ap()[:, :], wdbf[bi][:, t, :],
                    hdn[t][:, ch * 512:(ch + 1) * 512],
                    start=(t == 0), stop=False))
            for dt in range(NDT):
                bld = lambda E, dt=dt, ch=ch, bank=bank, bi=bi: E.matmul(
                    pb[bank].ap()[:, :], wobf[bi][:, dt, :],
                    num[dt][:, ch * 512:(ch + 1) * 512],
                    start=False, stop=(dt == NDT - 1))
                if dt == NDT - 1:
                    fm[ch] = P.inc("tensor", bld, "t_f", 1)
                else:
                    P.raw("tensor", bld)
            P.wait("vector", "t_f", fm[ch])
            P.wait("vector", "d_xr", xv)
            P.wait("vector", "d_y", P.val("d_y") - 1)
            yv = P.inc("vector", lambda E, ch=ch, bank=bank: E.tensor_add(
                xre[ch % 2][:, :], xre[ch % 2][:, :], pb[bank].ap()[:, :]),
                "s_y", 1)
            P.wait("sync", "s_y", yv)
            P.inc("sync", lambda E, e2=e2, ch=ch: E.dma_start(
                out=o_y.ap()[e2 * 128:(e2 + 1) * 128, ch * 512:(ch + 1) * 512],
                in_=xre[ch % 2][:, :]), "d_y", 16)
        wlast[id(wdbf[bi])] = ("t_f", fm[NCH - 1])
        wlast[id(wobf[bi])] = ("t_f", fm[NCH - 1])
        if stop_after == f"fin{e2}":
            P.wait("sync", "d_y", P.val("d_y"))
            return _finish()
    P.wait("sync", "d_y", P.val("d_y"))
    cxC2.close()
    cxH2.close()
    cxH.close()

    with nc.Block() as block:
        P.realize(block)
    return nc


_programs = {}
_token = int.from_bytes(os.urandom(3), "little") + 7

def _get_program(mode):
    if mode not in _programs:
        _programs[mode] = build_program(mode)
    return _programs[mode]


def _host_consts():
    consts = np.zeros((128, 8), np.float32)
    consts[:, 1] = 1.0
    consts[:, 2] = INV127
    cbf = np.zeros((128, 4), ml_dtypes.bfloat16)
    cbf[:, 0] = 1.0
    cbf[:64, 1] = 1.0
    cbf[64:, 2] = 1.0
    onesrow = np.ones((1, 128), np.float32)
    sel = np.zeros((16, NJ * 128), ml_dtypes.bfloat16)
    for j in range(NJ):
        sel[2 * j, j * 128:j * 128 + 64] = 1.0
        sel[2 * j + 1, j * 128 + 64:(j + 1) * 128] = 1.0
    return consts, cbf, onesrow, sel


def _tiles_t(w):
    E = w.shape[0]
    return np.ascontiguousarray(
        w.T.reshape(w.shape[1], E // 128, 128).transpose(1, 0, 2))


def kernel(L0_attn_x, L0_attn_W_in, L0_attn_W_out, L0_ffn_w_gate, L0_ffn_w_up,
           L0_ffn_w_down):
    x = np.ascontiguousarray(np.asarray(L0_attn_x, np.float32).reshape(16384, D))
    W_in = np.asarray(L0_attn_W_in, np.float32)
    W_out = np.asarray(L0_attn_W_out, np.float32)
    wg = np.asarray(L0_ffn_w_gate, np.int32)
    wu = np.asarray(L0_ffn_w_up, np.int32)
    wd = np.asarray(L0_ffn_w_down, np.int32)

    consts, cbf, onesrow, sel = _host_consts()
    w_qkv = _tiles_t(W_in)
    w_kv = np.ascontiguousarray(w_qkv[8:24])
    w_out_t = _tiles_t(W_out)
    w_g = _tiles_t(wg)
    w_u = _tiles_t(wu)
    w_d = _tiles_t(wd)
    xT_chunks = [np.ascontiguousarray(x[c * S:(c + 1) * S, :].T)
                 for c in range(NCORES)]

    global _token
    _token += 1
    tok = _token
    nc2 = _get_program("fused")
    in2 = []
    for c in range(NCORES):
        pair = c // 2
        if c % 2 == 0:
            r0, r1 = 0, 2048 * pair
        else:
            r0, r1 = 2048 * pair, 4096 * pair
        ntiles = (r1 - r0) // 512
        xr = np.zeros((D, 12 * 512), np.float32)
        if ntiles > 0:
            xr[:, :r1 - r0] = x[r0:r1, :].T
        rp = np.zeros((1, 8), np.int32)
        rp[0, 0] = ntiles
        rp[0, 1] = 1 - (c % 2)
        rp[0, 2] = tok
        in2.append({"xT": xT_chunks[c], "xRT": np.ascontiguousarray(xr),
                    "rparam": rp, "consts": consts, "cbf": cbf,
                    "onesrow": onesrow, "sel": sel, "w_qkv": w_qkv,
                    "w_out": w_out_t, "w_g": w_g, "w_u": w_u, "w_d": w_d})
    res2 = run_bass_kernel_spmd(nc2, in2, core_ids=list(range(NCORES)))
    y = np.concatenate([res2.results[c]["yT"].T for c in range(NCORES)], axis=0)
    return np.ascontiguousarray(y).reshape(1, 16384, D)
